# version 06
# baseline (speedup 1.0000x reference)
import contextlib
import numpy as np
import ml_dtypes
import concourse.bass as bass
from concourse.bass import broadcast_tensor_aps
import concourse.mybir as mybir
from concourse.bass_utils import run_bass_kernel_spmd

F32 = mybir.dt.float32
F16 = mybir.dt.float16
BF16 = mybir.dt.bfloat16
F8 = mybir.dt.float8e4
AF = mybir.ActivationFunctionType
ALU = mybir.AluOpType
AX = mybir.AxisListType.X

D = 1024
H = 16
NCP = 8
SQ = 2048
SH = 2050
SP = 2064
NT = 16
DHID = 4096
NCH = 32
WD = 256
NW = 8
C1 = float(np.float32(1.5 * 2 ** 23))
C2 = float(np.float32(1.5 * 2 ** 52))
EPS_ATTN = 1e-6
EPS_LN = 1e-5
QCLIP = 1e-5
INV_D = 1.0 / D
WNRM = 1.0 / (D * DHID)

_CN_DEBUG = None
KVZ_LIMIT = 99


def build_program(stage=4):
    nc = bass.Bass("TRN2", target_bir_lowering=False,
                   detect_race_conditions=False)

    x_d = nc.dram_tensor("x", [SH, D], F32, kind="ExternalInput").ap()
    wq_d = nc.dram_tensor("wq", [H, 64, 64], F32, kind="ExternalInput").ap()
    wk_d = nc.dram_tensor("wk", [H, 64, 64], F32, kind="ExternalInput").ap()
    wv_d = nc.dram_tensor("wv", [H, 64, 64], F32, kind="ExternalInput").ap()
    wo_d = nc.dram_tensor("wo", [D, D], F32, kind="ExternalInput").ap()
    w1t_d = nc.dram_tensor("w1t", [D, DHID], F32, kind="ExternalInput").ap()
    w2t_d = nc.dram_tensor("w2t", [DHID, D], F32, kind="ExternalInput").ap()
    cw_d = nc.dram_tensor("cw", [DHID, 3], F32, kind="ExternalInput").ap()
    cb_d = nc.dram_tensor("cb", [DHID], F32, kind="ExternalInput").ap()
    mask_d = nc.dram_tensor("mask", [128, 3], F32, kind="ExternalInput").ap()
    out_d = nc.dram_tensor("out", [SQ, D], F32, kind="ExternalOutput").ap()
    if stage != 4:
        dbg_d = nc.dram_tensor("dbg", [128, 1040], F32, kind="ExternalOutput").ap()
        dbg2_d = nc.dram_tensor("dbg2", [SP, D], F32, kind="ExternalOutput").ap()

    xbf_d = nc.dram_tensor("xbf_s", [SP, D], BF16, kind="Internal").ap()
    xq_dd = nc.dram_tensor("xq_s", [SP, D], BF16, kind="Internal").ap()
    ctx_dd = nc.dram_tensor("ctx_s", [SP, D], BF16, kind="Internal").ap()
    kvz_in = nc.dram_tensor("kvz_in", [128, 520], F32, kind="Internal").ap()
    kvz_out = nc.dram_tensor("kvz_out", [128, 520], F32, kind="Internal").ap()
    brow_d = nc.dram_tensor("brow_s", [1, SH], F32, kind="Internal").ap()
    arow_d = nc.dram_tensor("arow_s", [1, SH + 30], F32, kind="Internal").ap()
    arow_f = arow_d.rearrange("o s -> (o s)")

    identf_d = nc.inline_tensor(np.eye(128, dtype=np.float32), "identfsrc").ap()

    CN = {}

    def dinc(s):
        CN[s] = CN.get(s, 0) + 16
        return CN[s]

    def einc(s):
        CN[s] = CN.get(s, 0) + 1
        return CN[s]

    def cur(s):
        return CN.get(s, 0)

    global _CN_DEBUG
    _CN_DEBUG = CN
    with contextlib.ExitStack() as PERS:
        EP = PERS.enter_context

        def at(name, shape, dtype, off):
            return nc.alloc_sbuf_tensor_at(name, shape, dtype, offset=off + 2048)

        w1q8 = [at(f"w1q{j}", [128, DHID], F8, 16384 + 4096 * j) for j in range(8)]
        w2q8 = [at(f"w2q{c}", [128, D], F8, 49152 + 1024 * c) for c in range(NCH)]
        fmB = [at(f"fmB{j}", [128, SP], BF16, 81920 + 4128 * j) for j in range(8)]
        wo_bf = [at(f"wobf{j}", [128, D], BF16, 81920 + 2048 * j)
                 for j in range(8)]
        identf = at("identf", [128, 128], F32, 114944)
        maskc = at("maskc", [128, 3], F32, 115456)
        cc = at("cc", [128, 6], F32, 224160)
        cols = at("cols", [128, 32], F32, 115488)
        betac = at("betac", [128, 18], F32, 115616)
        wr = at("wr", [128, 2], F32, 115712)
        wsum2 = at("wsum2", [2, 1], F32, 115744)
        wrow2 = at("wrow2", [1, 2], F32, 115776)
        bhrow = at("bhrow", [1, 2], F32, 115808)
        monec = at("monec", [1, 128], F32, 115840)
        cwcol = at("cwcol", [128, 3 * NCH], F32, 116352)
        cbcol = at("cbcol", [128, NCH], F32, 116736)
        betacT = at("betacT", [16, 128], F32, 116864)
        bhalo = at("bhalo", [2, 1], F32, 117376)
        y1t = [at(f"y1t{k}", [128, D], F16, 117408 + 2048 * k) for k in range(NT)]
        y1h = at("y1h", [2, D], F16, 150176)
        kft = at("kft", [128, H * 64], BF16, 117408)
        v65 = at("v65", [128, H * 65], BF16, 119456)
        phe = at("phe", [128, 512], F32, 121568)
        phm = at("phm", [128, 512], F32, 123616)
        ctxt = [at(f"ctxt{i}", [128, D], BF16, 125664 + 2048 * i)
                for i in range(2)]
        qdg = [at(f"qdg{p}", [128, 128], BF16, 129760 + 256 * p)
               for p in range(NCP)]
        kdg = [at(f"kdg{p}", [128, 128], BF16, 131808 + 256 * p)
               for p in range(NCP)]
        vdg = [at(f"vdg{p}", [128, 128], BF16, 133856 + 256 * p)
               for p in range(NCP)]
        kvdg = [at(f"kvdg{p}", [128, 130], BF16, 135904 + 288 * p)
                for p in range(NCP)]
        wqkv_st = at("wqkv_st", [64, D], F32, 138208)
        kvzsb = at("kvzsb", [128, 520], F32, 142304)
        Rt = at("Rt", [128, 16], F32, 144416)
        y1f = [at(f"y1f{i}", [128, D], F32, 152224 + 4096 * i) for i in range(2)]
        outt = [at(f"outt{i}", [128, D], F32, 160416 + 4096 * i) for i in range(2)]
        scr = at("scr", [128, D], F32, 168608)
        wst = [at(f"wst{i}", [128, 512], F32, 152224 + 2048 * i) for i in range(2)]
        wqs = [at(f"wqs{i}", [128, 512], F32, 156320 + 2048 * i) for i in range(2)]
        fmA = [at(f"fmA{j}", [128, SP], BF16, 172704 + 4128 * j) for j in range(8)]
        ybuf = at("ybuf", [128, NCH * (WD + 1)], F32, 172704)
        xts = [at(f"xts{i}", [128, D], F32, 205728 + 4096 * i) for i in range(2)]
        xth = at("xth", [2, D], F32, 213920)
        castt = [at(f"castt{i}", [128, D], BF16, 218016 + 2048 * i)
                 for i in range(2)]
        xqt = castt
        zt14 = at("zt14", [16, D], BF16, 222112)
        Bwin = at("Bwin", [128, WD + 4], F32, 205728)
        S2t = at("S2t", [128, WD + 1], F32, 206784)
        Mx = at("Mx", [128, WD + 1], F32, 207840)
        Mn = at("Mn", [128, WD + 1], F32, 208896)
        hbuf = at("hbuf", [128, WD + 3], F32, 209952)
        hbuf2 = at("hbuf2", [128, WD + 3], F32, 224192)
        tc1 = at("tc1", [128, WD + 1], F32, 211008)
        tc2 = at("tc2", [128, WD + 1], F32, 212064)
        tc3 = at("tc3", [128, WD + 1], F32, 213120)
        pq = at("pq", [128, WD + 1], F32, 214176)
        pq2 = at("pq2", [128, WD + 1], F32, 225248)
        qr32 = at("qr32", [128, WD + 1], F32, 215232)
        qr64 = at("qr64", [128, WD + 1], F32, 216288)
        qwt = at("qwt", [128, WD + 1], F32, 217344)
        xq2 = [at(f"xq2_{i}", [128, WD + 1], BF16, 218400 + 544 * i)
               for i in range(2)]
        aseg = at("aseg", [1, 384], F32, 219488)
        s2row = at("s2row", [1, 384], F32, 221024)
        srow = at("srow", [1, 384], F32, 222560)
        amx = at("amx", [128, 4], F32, 224096)
        gcols = at("gcols", [128, 2], F32, 224128)

        PS = [EP(nc.psum_tensor(f"ps{i}", [128, 512], F32)) for i in range(8)]

        DL = EP(nc.semaphore("DL"))
        DG = EP(nc.semaphore("DG"))
        DX = [EP(nc.semaphore("DX0")), EP(nc.semaphore("DX1"))]
        DST = EP(nc.semaphore("DST"))
        DW = [EP(nc.semaphore("DW0")), EP(nc.semaphore("DW1"))]
        DR = [EP(nc.semaphore("DR0")), EP(nc.semaphore("DR1"))]
        DXH = EP(nc.semaphore("DXH"))
        DXQ = EP(nc.semaphore("DXQ"))
        DCX = EP(nc.semaphore("DCX"))
        DSR = EP(nc.semaphore("DSR"))
        DGC = EP(nc.semaphore("DGC"))
        DGA = EP(nc.semaphore("DGA"))
        DQKV = EP(nc.semaphore("DQKV"))
        BR = EP(nc.semaphore("BR"))
        TQ = EP(nc.semaphore("TQ"))
        VQ = EP(nc.semaphore("VQ"))
        SQs = EP(nc.semaphore("SQs"))
        GQ = EP(nc.semaphore("GQ"))
        CSs = EP(nc.semaphore("CSs"))
        block = EP(nc.Block())

        @block.sync
        def _(sy):
            sy.dma_start(out=identf[:, :], in_=identf_d).then_inc(DL, 16)
            sy.dma_start(out=maskc[:, :], in_=mask_d).then_inc(DL, 16)
            sy.dma_start(out=cwcol[:, :].rearrange("p (k t) -> p k t", t=3),
                         in_=cw_d.rearrange("(k p) t -> p k t", p=128)).then_inc(DL, 16)
            sy.dma_start(out=cbcol[:, :],
                         in_=cb_d.rearrange("(k p) -> p k", p=128),
                         allow_slow_non_contiguous=True).then_inc(DL, 16)
        for _i in range(4):
            dinc("DL")
        v_cast = []
        for t in range(17):
            @block.sync
            def _(sy, t=t):
                if t >= 2:
                    sy.wait_ge(VQ, v_cast[t - 2])
                if t < 16:
                    sy.dma_start(out=xts[t % 2][:, :],
                                 in_=x_d[128 * t:128 * t + 128, :]).then_inc(DX[t % 2], 16)
                else:
                    sy.dma_start(out=xth[:, :], in_=x_d[2048:2050, :]).then_inc(DX[t % 2], 16)
            v_xl = dinc(f"DX{t % 2}")
            @block.vector
            def _(v, t=t, v_xl=v_xl):
                v.wait_ge(DX[t % 2], v_xl)
                if t >= 2:
                    v.wait_ge(DST, 16 * (t - 1))
                p = 128 if t < 16 else 2
                src = xts[t % 2][:, :] if t < 16 else xth[:, :]
                v.tensor_copy(castt[t % 2][0:p, :], src).then_inc(VQ, 1)
            v_cast.append(einc("VQ"))
            @block.sync
            def _(sy, t=t):
                sy.wait_ge(VQ, v_cast[t])
                p = 128 if t < 16 else 2
                sy.dma_start(out=xbf_d[128 * t:128 * t + p, :],
                             in_=castt[t % 2][0:p, :]).then_inc(DST, 16)
            dinc("DST")

        @block.vector
        def _(v):
            v.memset(cols[:, 0:1], 1.0)
            v.memset(cols[:, 29:30], -1.0)
            v.memset(monec[:, :], 1.0)
            v.memset(wr[:, 0:2], 0.0)
            v.memset(cc[:, 0:1], 0.5)
            v.memset(cc[:, 1:2], -0.5)
            v.memset(cc[:, 2:3], 127.0)
            v.memset(cc[:, 3:4], 1.0 / 127.0)
            v.memset(cc[:, 4:5], INV_D)
            v.memset(cc[:, 5:6], EPS_LN)
            v.memset(zt14[:, :], 0.0)
            v.memset(v65[:, :], 0.0)
            ins = None
            for hh in range(H):
                ins = v.memset(v65[:, 65 * hh + 64:65 * hh + 65], 1.0)
            ins = v.tensor_scalar(cols[:, 1:2], cols[:, 0:1], C2, C2,
                                  ALU.add, ALU.subtract)
            ins.then_inc(VQ, 1)
        v_const = einc("VQ")

        @block.sync
        def _(sy):
            sy.wait_ge(VQ, v_const)
            sy.dma_start(out=xbf_d[2050:2064, :], in_=zt14[0:14, :]).then_inc(DL, 16)
        v_xpad = dinc("DL")
        v_dst_all = cur("DST")

        v_dg = []
        for proj, (wsrc, dgt) in enumerate(((wq_d, qdg), (wk_d, kdg), (wv_d, vdg))):
            @block.sync
            def _(sy, wsrc=wsrc, proj=proj):
                if proj > 0:
                    sy.wait_ge(VQ, v_dg[-1])
                sy.dma_start(out=wqkv_st[:, :].rearrange("d (h e) -> d h e", e=64),
                             in_=wsrc.rearrange("h d e -> d h e")).then_inc(DQKV, 16)
            v_w = dinc("DQKV")
            @block.vector
            def _(v, dgt=dgt, v_w=v_w):
                v.wait_ge(DQKV, v_w)
                ins = None
                for cp in range(NCP):
                    v.memset(dgt[cp][:, :], 0.0)
                    v.tensor_copy(dgt[cp][0:64, 0:64],
                                  wqkv_st[:, 128 * cp:128 * cp + 64])
                    ins = v.tensor_copy(dgt[cp][64:128, 64:128],
                                        wqkv_st[:, 128 * cp + 64:128 * cp + 128])
                ins.then_inc(VQ, 1)
            v_dg.append(einc("VQ"))
        v_dg_done = v_dg[-1]

        @block.sync
        def _(sy):
            sy.wait_ge(DL, v_xpad)
            sy.wait_ge(DST, v_dst_all)
            for j in range(8):
                sy.dma_start_transpose(
                    fmA[j][:, :], xbf_d[:, 128 * j:128 * j + 128]).then_inc(DL, 16)
        for j in range(8):
            dinc("DL")
        @block.sync
        def _(sy, tot=cur("DL")):
            sy.wait_ge(DL, tot)
            sy.sem_inc(BR, 1)
        br1 = einc("BR")

        pw = {"red": [], "ga": [], "wq": [], "mcols": None}

        def _pwA(wi, pc):
            def go():
                k = len(pw["red"])
                @block.sync
                def _(sy, wi=wi, pc=pc, k=k):
                    if k >= 2:
                        sy.wait_ge(VQ, pw["red"][k - 2])
                    if wi == 0:
                        s_ = w1t_d[128 * (pc // 8):128 * (pc // 8) + 128,
                                   512 * (pc % 8):512 * (pc % 8) + 512]
                    else:
                        s_ = w2t_d[128 * (pc // 2):128 * (pc // 2) + 128,
                                   512 * (pc % 2):512 * (pc % 2) + 512]
                    sy.dma_start(out=wst[k % 2][:, :], in_=s_).then_inc(DW[k % 2], 16)
                v_wl = dinc(f"DW{k % 2}")
                csl = (slice(9, 10) if k % 2 == 0 else slice(30, 31))
                @block.vector
                def _(v, wi=wi, k=k, v_wl=v_wl, csl=csl,
                      gprev=(pw["ga"][k - 2] if k >= 2 else None)):
                    v.wait_ge(DW[k % 2], v_wl)
                    if gprev is not None:
                        v.wait_ge(GQ, gprev)
                    v.tensor_reduce(cols[:, csl], wst[k % 2][:, :], AX, ALU.add,
                                    apply_absolute_value=True).then_inc(VQ, 1)
                pw["red"].append(einc("VQ"))
                @block.gpsimd
                def _(g, wi=wi, csl=csl, vr=pw["red"][-1]):
                    g.wait_ge(VQ, vr)
                    g.tensor_tensor(wr[:, wi:wi + 1], wr[:, wi:wi + 1],
                                    cols[:, csl], ALU.add).then_inc(GQ, 1)
                pw["ga"].append(einc("GQ"))
            return go

        def _pw_mcols():
            @block.tensor
            def _(te):
                te.wait_ge(GQ, pw["ga"][-1])
                te.wait_ge(VQ, pw["qf_done"])
                te.transpose(PS[0][0:2, 0:128], wr[:, 0:2],
                             identf[:, :]).then_inc(TQ, 1)
            t_wrT = einc("TQ")
            @block.vector
            def _(v, t_wrT=t_wrT):
                v.wait_ge(TQ, t_wrT)
                v.reduce_sum(wsum2[0:2, 0:1], PS[0][0:2, 0:128],
                             AX).then_inc(VQ, 1)
            v_ws = einc("VQ")
            @block.tensor
            def _(te, v_ws=v_ws):
                te.wait_ge(VQ, v_ws)
                te.transpose(PS[0][0:1, 0:2], wsum2[0:2, 0:1],
                             identf[0:2, 0:2]).then_inc(TQ, 1)
            t_wsT = einc("TQ")
            @block.vector
            def _(v, t_wsT=t_wsT):
                v.wait_ge(TQ, t_wsT)
                v.tensor_copy(wrow2[0:1, 0:2], PS[0][0:1, 0:2]).then_inc(VQ, 1)
            v_wr2 = einc("VQ")
            @block.tensor
            def _(te, v_wr2=v_wr2):
                te.wait_ge(VQ, v_wr2)
                te.matmul(PS[0][:, 0:2], monec[0:1, :], wrow2[0:1, 0:2],
                          start=True, stop=True).then_inc(TQ, 1)
            t_wbc = einc("TQ")
            @block.vector
            def _(v, t_wbc=t_wbc):
                v.wait_ge(TQ, t_wbc)
                v.tensor_scalar(cols[:, 2:4], PS[0][:, 0:2], WNRM, QCLIP,
                                ALU.mult, ALU.max).then_inc(VQ, 1)
            v_m12 = einc("VQ")
            @block.scalar
            def _(s, v_m12=v_m12):
                s.wait_ge(VQ, v_m12)
                s.activation(cols[:, 4:5], cols[:, 2:3], AF.Identity,
                             scale=cc[:, 0:1])
                s.activation(cols[:, 5:6], cols[:, 2:3], AF.Identity,
                             scale=cc[:, 1:2])
                s.activation(cols[:, 6:7], cols[:, 3:4], AF.Identity,
                             scale=cc[:, 0:1])
                s.activation(cols[:, 7:8], cols[:, 3:4], AF.Identity,
                             scale=cc[:, 1:2])
                s.activation(cols[:, 24:25], cols[:, 3:4], AF.Identity,
                             scale=cc[:, 3:4])
                s.activation(cols[:, 8:9], cols[:, 2:3], AF.Identity,
                             scale=cc[:, 3:4]).then_inc(SQs, 1)
            pw["mcols"] = einc("SQ")

        def _pwB(wi, pc):
            def go():
                k = len(pw["wq"])
                @block.sync
                def _(sy, wi=wi, pc=pc, k=k):
                    if k >= 2:
                        sy.wait_ge(VQ, pw["wq"][k - 2])
                    else:
                        sy.wait_ge(VQ, pw["red"][-1])
                    if wi == 0:
                        s_ = w1t_d[128 * (pc // 8):128 * (pc // 8) + 128,
                                   512 * (pc % 8):512 * (pc % 8) + 512]
                    else:
                        s_ = w2t_d[128 * (pc // 2):128 * (pc // 2) + 128,
                                   512 * (pc % 2):512 * (pc % 2) + 512]
                    sy.dma_start(out=wst[k % 2][:, :], in_=s_).then_inc(DW[k % 2], 16)
                v_wl = dinc(f"DW{k % 2}")
                if wi == 0:
                    dst = w1q8[pc // 8][:, 512 * (pc % 8):512 * (pc % 8) + 512]
                    @block.scalar
                    def _(s, pc=pc, k=k, v_wl=v_wl):
                        s.wait_ge(DW[k % 2], v_wl)
                        s.wait_ge(SQs, pw["mcols"])
                        if pw["wq"]:
                            s.wait_ge(VQ, pw["wq"][-1])
                        s.activation(wqs[0][:, :], wst[k % 2][:, :],
                                     AF.Sign, bias=cols[:, 5:6])
                        s.activation(wqs[1][:, :], wst[k % 2][:, :],
                                     AF.Sign, bias=cols[:, 4:5]).then_inc(SQs, 1)
                    s_w = einc("SQ")
                    @block.vector
                    def _(v, dst=dst, s_w=s_w):
                        v.wait_ge(SQs, s_w)
                        v.tensor_tensor(wqs[0][:, :], wqs[0][:, :], wqs[1][:, :],
                                        ALU.add)
                        v.tensor_scalar(dst, wqs[0][:, :], 0.5, None,
                                        ALU.mult).then_inc(VQ, 1)
                    pw["wq"].append(einc("VQ"))
                else:
                    dst = w2q8[pc // 2][:, 512 * (pc % 2):512 * (pc % 2) + 512]
                    @block.vector
                    def _(v, dst=dst, k=k, v_wl=v_wl):
                        v.wait_ge(DW[k % 2], v_wl)
                        v.wait_ge(SQs, pw["mcols"])
                        v.tensor_scalar(wqs[0][:, :], wst[k % 2][:, :],
                                        cols[:, 6:7], None, ALU.is_gt)
                        v.tensor_scalar(wqs[1][:, :], wst[k % 2][:, :],
                                        cols[:, 7:8], None, ALU.is_lt)
                        v.tensor_tensor(dst, wqs[0][:, :], wqs[1][:, :],
                                        ALU.subtract).then_inc(VQ, 1)
                    pw["wq"].append(einc("VQ"))
            return go

        pw_steps = []
        for wi in range(2):
            for pc in range(64):
                pw_steps.append(_pwA(wi, pc))
        pw_steps.append(_pw_mcols)
        for wi in range(2):
            for pc in range(64):
                pw_steps.append(_pwB(wi, pc))

        def drain_pw(n):
            for _ in range(n):
                if pw_steps:
                    pw_steps.pop(0)()

        qchunks = [(0, 512), (512, 512), (1024, 512), (1536, 512), (2048, 2)]
        bank_free_q = [None, None]
        v_qf_done = None
        for cp in range(NCP):
            for ci, (c0, cn) in enumerate(qchunks):
                idx = cp * len(qchunks) + ci
                bk = idx % 2
                need = bank_free_q[bk]
                @block.tensor
                def _(te, cp=cp, c0=c0, cn=cn, bk=bk, idx=idx, need=need):
                    if idx == 0:
                        te.wait_ge(BR, br1)
                        te.wait_ge(VQ, v_dg[0])
                    if need is not None:
                        te.wait_ge(VQ, need)
                    te.matmul(PS[bk][:, 0:cn], qdg[cp][:, :],
                              fmA[cp][:, c0:c0 + cn],
                              start=True, stop=True).then_inc(TQ, 1)
                t_q = einc("TQ")
                prev_stt = v_qf_done
                @block.scalar
                def _(s, t_q=t_q, cn=cn, bk=bk, prev_stt=prev_stt):
                    s.wait_ge(TQ, t_q)
                    if prev_stt is not None:
                        s.wait_ge(VQ, prev_stt)
                    s.activation(phm[:, 0:cn], PS[bk][:, 0:cn], AF.Relu,
                                 scale=cols[:, 29:30])
                    s.activation(phe[:, 0:cn], phm[:, 0:cn], AF.Exp,
                                 scale=cols[:, 29:30]).then_inc(SQs, 1)
                s_q = einc("SQ")
                @block.vector
                def _(v, s_q=s_q, cn=cn, bk=bk, cp=cp, c0=c0):
                    v.wait_ge(SQs, s_q)
                    v.scalar_tensor_tensor(fmB[cp][:, c0:c0 + cn], PS[bk][:, 0:cn],
                                           0.0, phe[:, 0:cn],
                                           ALU.max, ALU.add).then_inc(VQ, 1)
                v_qf_done = bank_free_q[bk] = einc("VQ")
                drain_pw(2)
        pw["qf_done"] = v_qf_done

        v_kvevac = []
        t_kvz_list = []
        for t in range(17):
            p = 128 if t < 16 else 2
            @block.tensor
            def _(te, t=t, p=p):
                if t == 0:
                    te.wait_ge(VQ, v_dg_done)
                    te.wait_ge(BR, br1)
                if t >= 1:
                    te.wait_ge(VQ, v_kvevac[t - 1])
                ins = None
                for cp in range(NCP):
                    kb = PS[2 + cp // 4]
                    vb = PS[4 + cp // 4]
                    off = 128 * (cp % 4)
                    te.matmul(kb[0:p, off:off + 128],
                              fmA[cp][:, 128 * t:128 * t + p],
                              kdg[cp][:, :], start=True, stop=True)
                    ins = te.matmul(vb[0:p, off:off + 128],
                                    fmA[cp][:, 128 * t:128 * t + p],
                                    vdg[cp][:, :], start=True, stop=True)
                ins.then_inc(TQ, 1)
            tt_kv = einc("TQ")
            if t >= 1 and t - 1 < KVZ_LIMIT:
                tp_prev = t - 1
                @block.tensor
                def _(te, tp=tp_prev):
                    te.wait_ge(VQ, v_kvevac[tp])
                    pk = 1 if tp == 16 else (128 if tp < 16 else 2)
                    ins = None
                    for hh in range(H):
                        cp = hh // 2
                        bank = PS[6 + cp // 4]
                        colo = 65 * (cp % 4)
                        tpos = (0, 64) if hh % 2 == 1 else None
                        po = 64 * (hh % 2)
                        ins = te.matmul(bank[po:po + 64, colo:colo + 65],
                                        kft[0:pk, 64 * hh:64 * hh + 64],
                                        v65[0:pk, 65 * hh:65 * hh + 65],
                                        start=(tp == 0 and hh in (0, 1, 8, 9)),
                                        stop=(tp == KVZ_LIMIT - 1 and hh in (6, 7, 14, 15)),
                                        skip_group_check=True,
                                        tile_position=tpos)
                    ins.then_inc(TQ, 1)
                t_kvz_list.append(einc("TQ"))
            gate = cur("TQ")
            prev_kstt = v_kvevac[t - 1] if t >= 1 else v_qf_done
            s_kf = []
            v_kstt = []
            for half in range(2):
                @block.scalar
                def _(s, gate=gate, p=p, half=half, prev=prev_kstt,
                      pv=(v_kstt[0] if half == 1 else None)):
                    s.wait_ge(TQ, gate)
                    if half == 0 and prev is not None:
                        s.wait_ge(VQ, prev)
                    if half == 1:
                        s.wait_ge(VQ, pv)
                    s.activation(phm[0:p, 0:512], PS[2 + half][0:p, 0:512],
                                 AF.Relu, scale=cols[0:p, 29:30])
                    s.activation(phe[0:p, 0:512], phm[0:p, 0:512],
                                 AF.Exp, scale=cols[0:p, 29:30]).then_inc(SQs, 1)
                s_kf.append(einc("SQ"))
                @block.vector
                def _(v, p=p, half=half, s_h=s_kf[half]):
                    v.wait_ge(SQs, s_h)
                    v.scalar_tensor_tensor(
                        kft[0:p, 512 * half:512 * half + 512],
                        PS[2 + half][0:p, 0:512], 0.0,
                        phe[0:p, 0:512], ALU.max, ALU.add).then_inc(VQ, 1)
                v_kstt.append(einc("VQ"))
            @block.vector
            def _(v, gate=gate, p=p, t=t):
                v.wait_ge(TQ, gate)
                ins = None
                for half in range(2):
                    for cp4 in range(4):
                        cp = 4 * half + cp4
                        for hh in range(2):
                            h_abs = 2 * cp + hh
                            ins = v.tensor_copy(
                                v65[0:p, 65 * h_abs:65 * h_abs + 64],
                                PS[4 + half][0:p,
                                             128 * cp4 + 64 * hh:128 * cp4 + 64 * hh + 64])
                if t == 0:
                    ins = v.memset(kft[0:1, :], 0.0)
                ins.then_inc(VQ, 1)
            v_kvevac.append(einc("VQ"))
            drain_pw(4)
        @block.tensor
        def _(te):
            if KVZ_LIMIT <= 16:
                te.engine_nop() if False else None
            te.wait_ge(VQ, v_kvevac[16])
            ins = None
            ins = None
            if KVZ_LIMIT > 16:
                for hh in range(H):
                    cp = hh // 2
                    bank = PS[6 + cp // 4]
                    colo = 65 * (cp % 4)
                    tpos = (0, 64) if hh % 2 == 1 else None
                    po = 64 * (hh % 2)
                    ins = te.matmul(bank[po:po + 64, colo:colo + 65],
                                    kft[0:1, 64 * hh:64 * hh + 64],
                                    v65[0:1, 65 * hh:65 * hh + 65],
                                    start=False, stop=(hh in (6, 7, 14, 15)),
                                    skip_group_check=True, tile_position=tpos)
                ins.then_inc(TQ, 1)
            else:
                ins = te.memset(cols[0:1, 31:32], 0.0) if False else None
                te.wait_ge(TQ, cur("TQ"))
                import concourse.mybir as _mb
                ins = te.matmul(PS[4][0:1, 0:1], cols[0:1, 0:1], cols[0:1, 0:1],
                                start=True, stop=True)
                ins.then_inc(TQ, 1)
        t_kvz_done = einc("TQ")

        @block.vector
        def _(v):
            v.wait_ge(TQ, t_kvz_done)
            v.tensor_copy(kvzsb[:, 0:260], PS[6][:, 0:260])
            v.tensor_copy(kvzsb[:, 260:520], PS[7][:, 0:260]).then_inc(VQ, 1)
        v_kvzevac = einc("VQ")
        @block.gpsimd
        def _(g):
            g.wait_ge(VQ, v_kvzevac)
            g.dma_start(out=kvz_in, in_=kvzsb[:, :]).then_inc(DG, 16)
        g_kvzout = dinc("DG")
        @block.gpsimd
        def _(g):
            g.wait_ge(DG, g_kvzout)
            g.collective_compute(
                "AllReduce", ALU.add, ins=[kvz_in], outs=[kvz_out],
                replica_groups=[[0, 1, 2, 3], [4, 5, 6, 7]]).then_inc(CSs, 1)
        c_ar = einc("CS")
        @block.gpsimd
        def _(g):
            g.wait_ge(CSs, c_ar)
            g.dma_start(out=kvzsb[:, :], in_=kvz_out).then_inc(DG, 16)
        g_kvzin = dinc("DG")
        @block.vector
        def _(v):
            v.wait_ge(DG, g_kvzin)
            ins = None
            for cp in range(NCP):
                c0 = 65 * cp
                v.memset(kvdg[cp][:, :], 0.0)
                v.tensor_copy(kvdg[cp][0:64, 0:64], kvzsb[0:64, c0:c0 + 64])
                v.tensor_copy(kvdg[cp][64:128, 64:128], kvzsb[64:128, c0:c0 + 64])
                v.tensor_copy(kvdg[cp][0:64, 128:129], kvzsb[0:64, c0 + 64:c0 + 65])
                ins = v.tensor_copy(kvdg[cp][64:128, 129:130],
                                    kvzsb[64:128, c0 + 64:c0 + 65])
            ins.then_inc(VQ, 1)
        v_kvdg = einc("VQ")

        if stage == 1:
            @block.gpsimd
            def _(g):
                g.wait_ge(VQ, v_kvdg)
                g.dma_start(out=dbg_d[:, 0:520], in_=kvzsb[:, :]).then_inc(DG, 16)
                g.dma_start(out=dbg2_d, in_=xbf_d).then_inc(DG, 16)
                dinc("DG"); dinc("DG")
                g.wait_ge(DG, cur("DG"))
                g.wait_ge(DL, cur("DL"))
            return nc

        v_ctxstage = []
        v_blk = {}
        t_num_ev = {}
        units_p4 = [(tt, cc_) for tt in range(17) for cc_ in range(NCP)]
        NIDX = len(units_p4)
        for bi in range(NIDX + 2):
            if bi < NIDX:
                t, cp = units_p4[bi]
                p = 128 if t < 16 else 2
                @block.tensor
                def _(te, bi=bi, t=t, cp=cp, p=p):
                    if bi == 0:
                        te.wait_ge(VQ, v_kvdg)
                        te.wait_ge(VQ, v_qf_done)
                        te.wait_ge(SQs, pw["mcols"])
                    if bi >= 4:
                        te.wait_ge(VQ, v_blk[bi - 2])
                    te.matmul(PS[bi % 4][0:p, 0:130],
                              fmB[cp][:, 128 * t:128 * t + p],
                              kvdg[cp][:, :], start=True, stop=True).then_inc(TQ, 1)
                t_num_ev[bi] = einc("TQ")
            @block.vector
            def _(v, bi=bi):
                ins = None
                if bi < NIDX:
                    v.wait_ge(TQ, t_num_ev[bi])
                    t, cp = units_p4[bi]
                    p = 128 if t < 16 else 2
                    sl = 4 * (bi % 2)
                    ins = v.tensor_scalar(Rt[0:p, sl:sl + 2],
                                          PS[bi % 4][0:p, 128:130],
                                          EPS_ATTN, None, ALU.add)
                if 0 <= bi - 1 < NIDX:
                    t, cp = units_p4[bi - 1]
                    p = 128 if t < 16 else 2
                    sl = 4 * ((bi - 1) % 2)
                    ins = v.reciprocal(Rt[0:p, sl + 2:sl + 4],
                                       Rt[0:p, sl:sl + 2])
                if bi - 2 >= 0:
                    t, cp = units_p4[bi - 2]
                    p = 128 if t < 16 else 2
                    sl = 4 * ((bi - 2) % 2)
                    if cp == 0 and t >= 2:
                        v.wait_ge(DCX, 16 * (t - 1))
                    for hh in range(2):
                        ins = v.tensor_scalar(
                            ctxt[t % 2][0:p,
                                        128 * cp + 64 * hh:128 * cp + 64 * hh + 64],
                            PS[(bi - 2) % 4][0:p, 64 * hh:64 * hh + 64],
                            Rt[0:p, sl + 2 + hh:sl + 3 + hh], None, ALU.mult)
                ins.then_inc(VQ, 1)
            v_blk[bi] = einc("VQ")
            if bi >= 9 and (bi - 9) % 8 == 0:
                tso = (bi - 9) // 8
                pso = 2 if tso == 16 else 128
                @block.sync
                def _(sy, tso=tso, pso=pso, vb=v_blk[bi]):
                    sy.wait_ge(VQ, vb)
                    sy.dma_start(out=ctx_dd[128 * tso:128 * tso + pso, :],
                                 in_=ctxt[tso % 2][0:pso, :]).then_inc(DCX, 16)
                v_ctxstage.append(dinc("DCX"))
            drain_pw(1)
        drain_pw(len(pw_steps))
        v_wq_done = pw["wq"][-1]
        @block.sync
        def _(sy):
            sy.dma_start(out=ctx_dd[2050:2064, :], in_=zt14[0:14, :]).then_inc(DL, 16)
        v_ctxpad = dinc("DL")
        @block.sync
        def _(sy, dcx_all=cur("DCX")):
            sy.wait_ge(DL, v_ctxpad)
            sy.wait_ge(DCX, dcx_all)
            for j in range(8):
                sy.dma_start_transpose(
                    fmA[j][:, :], ctx_dd[:, 128 * j:128 * j + 128]).then_inc(DL, 16)
        for j in range(8):
            dinc("DL")
        @block.sync
        def _(sy, tot=cur("DL")):
            sy.wait_ge(DL, tot)
            sy.sem_inc(BR, 1)
        br2 = einc("BR")

        if stage == 2:
            @block.gpsimd
            def _(g):
                g.wait_ge(BR, br2)
                g.dma_start(out=dbg2_d, in_=ctx_dd).then_inc(DG, 16)
                g.dma_start(out=dbg_d[:, 0:520], in_=kvzsb[:, :]).then_inc(DG, 16)
                dinc("DG"); dinc("DG")
                g.wait_ge(DG, cur("DG"))
                g.wait_ge(DL, cur("DL"))
            return nc

        v_wocast = []
        for j in range(8):
            @block.sync
            def _(sy, j=j):
                if j >= 2:
                    sy.wait_ge(VQ, v_wocast[j - 2])
                sy.dma_start(out=xts[j % 2][:, :],
                             in_=wo_d[128 * j:128 * j + 128, :]).then_inc(DR[j % 2], 16)
            v_wl = dinc(f"DR{j % 2}")
            @block.vector
            def _(v, j=j, v_wl=v_wl):
                v.wait_ge(DR[j % 2], v_wl)
                if j == 7:
                    v.wait_ge(BR, br2)
                v.tensor_copy(wo_bf[j][:, :], xts[j % 2][:, :]).then_inc(VQ, 1)
            v_wocast.append(einc("VQ"))
        v_wo_done = v_wocast[-1]

        units = [(k, 128) for k in range(NT)] + [(16, 2)]
        v_ln1_ev = {}
        v_xqstage = []
        for ui, (k, p) in enumerate(units):
            bko = 2 * (ui % 2)
            need = v_ln1_ev.get(ui - 2)
            @block.sync
            def _(sy, ui=ui, k=k, p=p):
                if ui == 0:
                    sy.wait_ge(VQ, v_wo_done)
                if ui >= 2:
                    sy.wait_ge(VQ, v_ln1_ev[ui - 2])
                if p == 128:
                    sy.dma_start(out=xts[ui % 2][:, :],
                                 in_=x_d[1 + 128 * k:1 + 128 * k + 128, :]).then_inc(DR[ui % 2], 16)
                else:
                    sy.dma_start(out=xth[:, :],
                                 in_=x_d[0:SH:2049, :]).then_inc(DXH, 16)
            v_xr = dinc(f"DR{ui % 2}") if p == 128 else dinc("DXH")
            @block.tensor
            def _(te, ui=ui, k=k, p=p, bko=bko, need=need):
                if ui == 0:
                    te.wait_ge(BR, br2)
                    te.wait_ge(VQ, v_wo_done)
                if need is not None:
                    te.wait_ge(VQ, need)
                ins = None
                for e in range(8):
                    if p == 128:
                        lh = fmA[e][:, 1 + 128 * k:1 + 128 * k + 128]
                    else:
                        lh = fmA[e][:, 0:SH:2049]
                    te.matmul(PS[bko][0:p, 0:512], lh, wo_bf[e][:, 0:512],
                              start=(e == 0), stop=(e == 7))
                    ins = te.matmul(PS[bko + 1][0:p, 0:512], lh, wo_bf[e][:, 512:1024],
                                    start=(e == 0), stop=(e == 7))
                ins.then_inc(TQ, 1)
            t_wo = einc("TQ")
            @block.vector
            def _(v, ui=ui, k=k, p=p, bko=bko, t_wo=t_wo, v_xr=v_xr):
                v.wait_ge(TQ, t_wo)
                v.wait_ge(DR[ui % 2] if p == 128 else DXH, v_xr)
                xsrc = xts[ui % 2] if p == 128 else xth
                v.scalar_tensor_tensor(y1f[ui % 2][0:p, 0:512], PS[bko][0:p, 0:512],
                                       1.0, xsrc[0:p, 0:512], ALU.mult, ALU.add,
                                       accum_out=cols[0:p, 10:11])
                v.scalar_tensor_tensor(y1f[ui % 2][0:p, 512:1024],
                                       PS[bko + 1][0:p, 0:512],
                                       1.0, xsrc[0:p, 512:1024], ALU.mult, ALU.add,
                                       accum_out=cols[0:p, 9:10])
                v.tensor_reduce(cols[0:p, 16:17], y1f[ui % 2][0:p, :], AX, ALU.max)
                v.tensor_reduce(cols[0:p, 17:18], y1f[ui % 2][0:p, :], AX,
                                ALU.min).then_inc(VQ, 1)
            v_res = einc("VQ")
            @block.scalar
            def _(s, ui=ui, p=p, v_res=v_res):
                s.wait_ge(VQ, v_res)
                s.activation(outt[(ui + 1) % 2][0:p, :], y1f[ui % 2][0:p, :],
                             AF.Square,
                             accum_out=cols[0:p, 11:12]).then_inc(SQs, 1)
            s_sq = einc("SQ")
            @block.gpsimd
            def _(g, p=p, s_sq=s_sq):
                g.wait_ge(SQs, s_sq)
                g.tensor_tensor(cols[0:p, 12:13], cols[0:p, 10:11],
                                cols[0:p, 9:10], ALU.add)
                g.tensor_tensor(cols[0:p, 12:13], cols[0:p, 12:13],
                                cc[0:p, 4:5], ALU.mult)
                g.tensor_tensor(cols[0:p, 26:27], cols[0:p, 12:13],
                                cols[0:p, 12:13], ALU.mult)
                g.tensor_tensor(cols[0:p, 11:12], cols[0:p, 11:12],
                                cc[0:p, 4:5], ALU.mult)
                g.tensor_tensor(cols[0:p, 13:14], cols[0:p, 11:12],
                                cols[0:p, 26:27], ALU.subtract)
                g.tensor_tensor(cols[0:p, 13:14], cols[0:p, 13:14],
                                cc[0:p, 5:6], ALU.add)
                g.tensor_tensor(cols[0:p, 18:19], cols[0:p, 16:17],
                                cols[0:p, 12:13], ALU.subtract)
                g.tensor_tensor(cols[0:p, 19:20], cols[0:p, 12:13],
                                cols[0:p, 17:18], ALU.subtract).then_inc(GQ, 1)
            g_st = einc("GQ")
            @block.scalar
            def _(s, p=p, g_st=g_st):
                s.wait_ge(GQ, g_st)
                s.activation(cols[0:p, 27:28], cols[0:p, 13:14],
                             AF.Sqrt).then_inc(SQs, 1)
            s_a = einc("SQ")
            @block.vector
            def _(v, p=p, s_a=s_a):
                v.wait_ge(SQs, s_a)
                v.tensor_tensor(cols[0:p, 20:21], cols[0:p, 18:19],
                                cols[0:p, 19:20], ALU.max)
                v.reciprocal(cols[0:p, 14:15], cols[0:p, 27:28]).then_inc(VQ, 1)
            v_a = einc("VQ")
            @block.gpsimd
            def _(g, p=p, v_a=v_a):
                g.wait_ge(VQ, v_a)
                g.tensor_tensor(cols[0:p, 21:22], cols[0:p, 20:21],
                                cols[0:p, 14:15], ALU.mult).then_inc(GQ, 1)
            g_am = einc("GQ")
            @block.vector
            def _(v, ui=ui, k=k, p=p, g_am=g_am):
                v.wait_ge(GQ, g_am)
                v.tensor_scalar(cols[0:p, 21:22], cols[0:p, 21:22], QCLIP, None,
                                ALU.max)
                dst = y1t[k][0:p, :] if p == 128 else y1h[0:p, :]
                v.tensor_copy(dst, y1f[ui % 2][0:p, :])
                v.reciprocal(cols[0:p, 25:26], cols[0:p, 21:22]).then_inc(VQ, 1)
            v_ir = einc("VQ")
            @block.gpsimd
            def _(g, ui=ui, k=k, p=p, v_ir=v_ir):
                g.wait_ge(VQ, v_ir)
                g.tensor_tensor(cols[0:p, 28:29], cols[0:p, 25:26],
                                cc[0:p, 2:3], ALU.mult)
                g.tensor_tensor(cols[0:p, 22:23], cols[0:p, 14:15],
                                cols[0:p, 28:29], ALU.mult)
                g.tensor_tensor(cols[0:p, 15:16], cols[0:p, 12:13],
                                cols[0:p, 14:15], ALU.mult)
                g.tensor_tensor(cols[0:p, 15:16], cols[0:p, 15:16],
                                cols[0:p, 29:30], ALU.mult)
                g.tensor_tensor(cols[0:p, 23:24], cols[0:p, 15:16],
                                cols[0:p, 28:29], ALU.mult)
                bc = betac[0:p, k:k + 1] if p == 128 else bhalo[0:2, 0:1]
                ins = g.tensor_tensor(bc, cols[0:p, 21:22], cols[0:p, 8:9],
                                      ALU.mult)
                if p == 2:
                    ins = g.tensor_tensor(bc, bc, maskc[0:2, 2:3], ALU.mult)
                ins.then_inc(GQ, 1)
            g_ab = einc("GQ")
            @block.vector
            def _(v, ui=ui, p=p, g_ab=g_ab):
                v.wait_ge(GQ, g_ab)
                v.tensor_scalar(scr[0:p, :], y1f[ui % 2][0:p, :],
                                cols[0:p, 22:23], cols[0:p, 23:24],
                                ALU.mult, ALU.add)
                v.tensor_scalar(outt[(ui + 1) % 2][0:p, :], scr[0:p, :], C1, C1,
                                ALU.add, ALU.subtract)
                v.tensor_scalar(y1f[(ui + 1) % 2][0:p, :], scr[0:p, :], C2, C2,
                                ALU.add, ALU.subtract).then_inc(VQ, 1)
            v_t2 = einc("VQ")
            @block.gpsimd
            def _(g, ui=ui, p=p, v_t2=v_t2):
                g.wait_ge(VQ, v_t2)
                g.tensor_tensor(scr[0:p, :], y1f[(ui + 1) % 2][0:p, :],
                                outt[(ui + 1) % 2][0:p, :],
                                ALU.subtract).then_inc(GQ, 1)
            g_w = einc("GQ")
            @block.vector
            def _(v, ui=ui, p=p, g_w=g_w):
                v.wait_ge(GQ, g_w)
                if ui >= 2:
                    v.wait_ge(DXQ, 16 * (ui - 1))
                v.scalar_tensor_tensor(xqt[ui % 2][0:p, :], scr[0:p, :],
                                       cols[0:p, 1:2],
                                       outt[(ui + 1) % 2][0:p, :],
                                       ALU.mult, ALU.add).then_inc(VQ, 1)
            v_ln1_ev[ui] = einc("VQ")
            @block.sync
            def _(sy, ui=ui, k=k, p=p):
                sy.wait_ge(VQ, v_ln1_ev[ui])
                if p == 128:
                    sy.dma_start(out=xq_dd[1 + 128 * k:1 + 128 * k + 128, :],
                                 in_=xqt[ui % 2][:, :]).then_inc(DXQ, 16)
                else:
                    sy.dma_start(out=xq_dd[0:SH:2049, :],
                                 in_=xqt[ui % 2][0:2, :]).then_inc(DXQ, 16)
            v_xqstage.append(dinc("DXQ"))
        v_ln1_done = v_ln1_ev[len(units) - 1]

        @block.tensor
        def _(te):
            te.wait_ge(VQ, v_ln1_done)
            te.transpose(PS[4][0:16, 0:128], betac[:, 0:16], identf[:, :])
            te.transpose(PS[5][0:1, 0:2], bhalo[0:2, 0:1],
                         identf[0:2, 0:2]).then_inc(TQ, 1)
        t_bT = einc("TQ")
        @block.vector
        def _(v):
            v.wait_ge(TQ, t_bT)
            v.tensor_copy(betacT[0:16, :], PS[4][0:16, 0:128])
            v.tensor_copy(bhrow[0:1, 0:2], PS[5][0:1, 0:2]).then_inc(VQ, 1)
        v_bT = einc("VQ")
        @block.sync
        def _(sy):
            sy.wait_ge(VQ, v_bT)
            sy.dma_start(out=brow_d[0:1, 1:2049], in_=betacT[0:16, :]).then_inc(DL, 16)
            sy.dma_start(out=brow_d[0:1, 0:SH:2049],
                         in_=bhrow[0:1, 0:2],
                         allow_slow_non_contiguous=True).then_inc(DL, 16)
        dinc("DL")
        v_brow = dinc("DL")

        @block.sync
        def _(sy, dxq_all=cur("DXQ")):
            sy.wait_ge(DXQ, dxq_all)
            sy.dma_start(out=xq_dd[2050:2064, :], in_=zt14[0:14, :]).then_inc(DL, 16)
        v_xqpad = dinc("DL")
        @block.sync
        def _(sy):
            sy.wait_ge(DL, v_xqpad)
            for j in range(8):
                sy.dma_start_transpose(
                    fmB[j][:, :], xq_dd[:, 128 * j:128 * j + 128]).then_inc(DL, 16)
        for j in range(8):
            dinc("DL")
        @block.sync
        def _(sy, tot=cur("DL")):
            sy.wait_ge(DL, tot)
            sy.sem_inc(BR, 1)
        br3 = einc("BR")

        if stage == 3:
            @block.gpsimd
            def _(g):
                g.wait_ge(BR, br3)
                g.dma_start(out=dbg2_d, in_=xq_dd).then_inc(DG, 16)
                g.dma_start(out=dbg_d[:, 0:16], in_=betac[:, 0:16]).then_inc(DG, 16)
                g.dma_start(out=dbg_d[:, 18:20], in_=cols[:, 2:4]).then_inc(DG, 16)
                g.dma_start(out=dbg_d[0:2, 20:21], in_=bhalo[0:2, 0:1],
                            allow_slow_non_contiguous=True).then_inc(DG, 16)
                g.dma_start(out=dbg_d[0:1, 22:24], in_=bhrow[0:1, 0:2]).then_inc(DG, 16)
                g.dma_start(out=dbg_d[0:1, 24:26],
                            in_=brow_d[0:1, 0:SH:2049],
                            allow_slow_non_contiguous=True).then_inc(DG, 16)
                dinc("DG"); dinc("DG"); dinc("DG"); dinc("DG"); dinc("DG"); dinc("DG")
                g.wait_ge(DG, cur("DG"))
                g.wait_ge(DL, cur("DL"))
            return nc


        WY = WD + 1
        v_w2ev = {}
        g_store = []
        for w in range(NW):
            lo = max(256 * w - 1, 0)
            hi = min(256 * w + WD + 2, SH)
            width = hi - lo
            c0h = 256 * w - lo
            @block.sync
            def _(sy, w=w, lo=lo, width=width):
                if w == 0:
                    sy.wait_ge(DL, v_brow)
                else:
                    sy.wait_ge(TQ, t_bw_prev)
                sy.dma_start(out=srow[0:1, 0:width],
                             in_=brow_d[0:1, lo:lo + width]).then_inc(DSR, 16)
            v_srw = dinc("DSR")
            @block.tensor
            def _(te, w=w, lo=lo, width=width, v_srw=v_srw):
                if w == 0:
                    te.wait_ge(BR, br3)
                    te.wait_ge(VQ, v_wq_done)
                te.wait_ge(DSR, v_srw)
                te.matmul(PS[6][:, 0:width], monec[0:1, :], srow[0:1, 0:width],
                          start=True, stop=True).then_inc(TQ, 1)
            t_bw = einc("TQ")
            t_bw_prev = t_bw
            @block.vector
            def _(v, t_bw=t_bw, width=width):
                v.wait_ge(TQ, t_bw)
                v.tensor_copy(Bwin[:, 0:width], PS[6][:, 0:width]).then_inc(VQ, 1)
            v_bw = einc("VQ")

            hbufs = [hbuf, hbuf2]
            v_e1 = {}
            v_mm = {}
            s_cv_hist = {}
            g_cv_prev = None
            for c in range(NCH):
                bk = c % 2
                need = v_e1.get(c - 2)
                @block.tensor
                def _(te, w=w, c=c, bk=bk, lo=lo, width=width, need=need,
                      v_bw=v_bw):
                    if c == 0:
                        te.wait_ge(VQ, v_bw)
                        if w == 0:
                            te.wait_ge(BR, br3)
                    if need is not None:
                        te.wait_ge(VQ, need)
                    ins = None
                    for dj in range(8):
                        ins = te.matmul(PS[bk][:, 0:width],
                                        w1q8[dj][:, 128 * c:128 * c + 128],
                                        fmB[dj][:, lo:lo + width],
                                        start=(dj == 0), stop=(dj == 7))
                    ins.then_inc(TQ, 1)
                t_g = einc("TQ")
                @block.vector
                def _(v, w=w, c=c, bk=bk, width=width, t_g=t_g,
                      s_cv_prev=s_cv_hist.get(c - 2)):
                    v.wait_ge(TQ, t_g)
                    if s_cv_prev is not None:
                        v.wait_ge(SQs, s_cv_prev)
                    ins = v.tensor_tensor(hbufs[c % 2][:, 0:width],
                                          PS[bk][:, 0:width],
                                          Bwin[:, 0:width], ALU.mult)
                    ins.then_inc(VQ, 1)
                v_e1[c] = einc("VQ")
                @block.scalar
                def _(s, w=w, c=c, c0h=c0h, v_e1c=v_e1[c], g_cv_prev=g_cv_prev):
                    s.wait_ge(VQ, v_e1c)
                    if g_cv_prev is not None:
                        s.wait_ge(GQ, g_cv_prev)
                    hb = hbufs[c % 2]
                    s.activation(tc1[:, 0:WY], hb[:, c0h + 1:c0h + 1 + WY],
                                 AF.Identity, scale=cwcol[:, 3 * c + 2:3 * c + 3],
                                 bias=cbcol[:, c:c + 1])
                    if c0h >= 1:
                        s.activation(tc2[:, 0:WY], hb[:, c0h - 1:c0h - 1 + WY],
                                     AF.Identity, scale=cwcol[:, 3 * c:3 * c + 1])
                    else:
                        s.activation(tc2[:, 1:WY], hb[:, 0:WY - 1],
                                     AF.Identity, scale=cwcol[:, 3 * c:3 * c + 1])
                        s.activation(tc2[:, 0:1], hb[:, 0:1],
                                     AF.Identity, scale=cwcol[:, 3 * c:3 * c + 1])
                    s.activation(tc3[:, 0:WY], hb[:, c0h:c0h + WY],
                                 AF.Identity,
                                 scale=cwcol[:, 3 * c + 1:3 * c + 2]).then_inc(SQs, 1)
                s_cv = einc("SQ")
                s_cv_hist[c] = s_cv
                @block.gpsimd
                def _(g, s_cv=s_cv):
                    g.wait_ge(SQs, s_cv)
                    g.tensor_tensor(tc1[:, 0:WY], tc1[:, 0:WY], tc2[:, 0:WY],
                                    ALU.add)
                    g.tensor_tensor(tc2[:, 0:WY], tc1[:, 0:WY], tc3[:, 0:WY],
                                    ALU.add).then_inc(GQ, 1)
                g_cv = einc("GQ")
                g_cv_prev = g_cv
                @block.scalar
                def _(s, c=c, g_cv=g_cv):
                    s.wait_ge(GQ, g_cv)
                    s.activation(ybuf[:, WY * c:WY * c + WY], tc2[:, 0:WY],
                                 AF.Silu).then_inc(SQs, 1)
                s_si = einc("SQ")
                @block.vector
                def _(v, c=c, s_si=s_si):
                    v.wait_ge(SQs, s_si)
                    ys = ybuf[:, WY * c:WY * c + WY]
                    if c == 0:
                        v.tensor_copy(Mx[:, 0:WY], ys)
                        v.tensor_copy(Mn[:, 0:WY], ys).then_inc(VQ, 1)
                    else:
                        v.tensor_tensor(Mx[:, 0:WY], Mx[:, 0:WY], ys, ALU.max)
                        v.tensor_tensor(Mn[:, 0:WY], Mn[:, 0:WY], ys,
                                        ALU.min).then_inc(VQ, 1)
                v_mm[c] = einc("VQ")
            @block.vector
            def _(v, vmm=v_mm[NCH - 1]):
                v.wait_ge(VQ, vmm)
                v.tensor_scalar(Mn[:, 0:WY], Mn[:, 0:WY], -1.0, None, ALU.mult)
                v.tensor_tensor(Mx[:, 0:WY], Mx[:, 0:WY], Mn[:, 0:WY],
                                ALU.max).then_inc(VQ, 1)
            v_am = einc("VQ")
            t_mT_last = None
            for b3, (cw3, wd3) in enumerate(((0, 128), (128, 128), (256, 1))):
                @block.tensor
                def _(te, b3=b3, cw3=cw3, wd3=wd3, v_am=v_am):
                    if b3 == 0:
                        te.wait_ge(VQ, v_am)
                    else:
                        te.wait_ge(VQ, cur("VQ"))
                    te.transpose(PS[6][0:wd3, 0:128], Mx[:, cw3:cw3 + wd3],
                                 identf[:, :]).then_inc(TQ, 1)
                t_mT = einc("TQ")
                @block.vector
                def _(v, b3=b3, wd3=wd3, t_mT=t_mT, w=w, prev_aw=(None if w == 0
                                                                  else v_aw_prev)):
                    v.wait_ge(TQ, t_mT)
                    if b3 == 0 and prev_aw is not None:
                        v.wait_ge(DGA, prev_aw)
                    v.reduce_max(amx[0:wd3, b3:b3 + 1], PS[6][0:wd3, 0:128],
                                 AX).then_inc(VQ, 1)
                einc("VQ")
            for b3 in range(2):
                @block.tensor
                def _(te, b3=b3):
                    te.wait_ge(VQ, cur("VQ"))
                    te.transpose(PS[6][0:1, 0:128], amx[:, b3:b3 + 1],
                                 identf[:, :]).then_inc(TQ, 1)
                t_aT = einc("TQ")
                @block.vector
                def _(v, b3=b3, t_aT=t_aT):
                    v.wait_ge(TQ, t_aT)
                    v.tensor_copy(aseg[0:1, 128 * b3:128 * b3 + 128],
                                  PS[6][0:1, 0:128]).then_inc(VQ, 1)
                einc("VQ")
            @block.vector
            def _(v):
                v.tensor_copy(aseg[0:1, 256:257], amx[0:1, 2:3])
                v.tensor_scalar(s2row[0:1, 0:256], aseg[0:1, 0:256], QCLIP,
                                None, ALU.max)
                v.tensor_scalar(s2row[0:1, 256:257], amx[0:1, 2:3], QCLIP,
                                None, ALU.max).then_inc(VQ, 1)
            v_s2 = einc("VQ")
            @block.gpsimd
            def _(g, v_s2=v_s2):
                g.wait_ge(VQ, v_s2)
                a2, b2 = broadcast_tensor_aps(s2row[0:1, 0:257], cc[0:1, 3:4])
                g.tensor_tensor(s2row[0:1, 0:257], a2, b2,
                                ALU.mult).then_inc(GQ, 1)
            g_s2c = einc("GQ")
            @block.vector
            def _(v, g_s2c=g_s2c):
                v.wait_ge(GQ, g_s2c)
                v.reciprocal(s2row[0:1, 0:257],
                             s2row[0:1, 0:257]).then_inc(VQ, 1)
            g_s2f = ("VQ", einc("VQ"))
            @block.sync
            def _(sy, w=w, v_s2=v_s2):
                sy.wait_ge(VQ, v_s2)
                sy.dma_start(out=arow_d[0:1, 256 * w:256 * w + 257],
                             in_=aseg[0:1, 0:257]).then_inc(DGA, 16)
            v_aw = dinc("DGA")
            v_aw_prev = v_aw
            @block.sync
            def _(sy, w=w, v_aw=v_aw, gprev=(None if w == 0 else v_w2ev.get((w - 1, 1)))):
                sy.wait_ge(DGA, v_aw)
                if gprev is not None:
                    sy.wait_ge(VQ, gprev)
                sy.dma_start(
                    out=gcols[:, 0:2],
                    in_=arow_f[1 + 256 * w:1 + 256 * w + 256].rearrange(
                        "(j p) -> p j", p=128),
                    allow_slow_non_contiguous=True).then_inc(DGC, 16)
            v_gc = dinc("DGC")
            @block.tensor
            def _(te, g_s2f=g_s2f):
                te.wait_ge(VQ, g_s2f[1])
                te.matmul(PS[6][:, 0:257], monec[0:1, :], s2row[0:1, 0:257],
                          start=True, stop=True).then_inc(TQ, 1)
            t_s2b = einc("TQ")
            @block.vector
            def _(v, t_s2b=t_s2b):
                v.wait_ge(TQ, t_s2b)
                v.tensor_copy(S2t[:, 0:WY], PS[6][:, 0:257]).then_inc(VQ, 1)
            v_s2t = einc("VQ")

            pqs = [pq, pq2]
            v_xq2 = {}
            t_w2 = {}
            v_r_hist = {}
            for c in range(NCH):
                @block.gpsimd
                def _(g, c=c, v_s2t=v_s2t, v_r_prev=v_r_hist.get(c - 2)):
                    g.wait_ge(VQ, v_s2t)
                    if v_r_prev is not None:
                        g.wait_ge(VQ, v_r_prev)
                    g.tensor_tensor(pqs[c % 2][:, 0:WY],
                                    ybuf[:, WY * c:WY * c + WY],
                                    S2t[:, 0:WY], ALU.mult).then_inc(GQ, 1)
                g_p = einc("GQ")
                @block.vector
                def _(v, c=c, g_p=g_p):
                    v.wait_ge(GQ, g_p)
                    v.tensor_scalar(qr32[:, 0:WY], pqs[c % 2][:, 0:WY], C1, C1,
                                    ALU.add, ALU.subtract)
                    v.tensor_scalar(qr64[:, 0:WY], pqs[c % 2][:, 0:WY], C2, C2,
                                    ALU.add, ALU.subtract).then_inc(VQ, 1)
                v_r = einc("VQ")
                v_r_hist[c] = v_r
                @block.gpsimd
                def _(g, v_r=v_r):
                    g.wait_ge(VQ, v_r)
                    g.tensor_tensor(qwt[:, 0:WY], qr64[:, 0:WY], qr32[:, 0:WY],
                                    ALU.subtract).then_inc(GQ, 1)
                g_w2 = einc("GQ")
                @block.vector
                def _(v, c=c, g_w2=g_w2):
                    v.wait_ge(GQ, g_w2)
                    if c >= 2:
                        v.wait_ge(TQ, t_w2[c - 2])
                    v.scalar_tensor_tensor(xq2[c % 2][:, 0:WY], qwt[:, 0:WY],
                                           cols[:, 1:2], qr32[:, 0:WY],
                                           ALU.mult, ALU.add).then_inc(VQ, 1)
                v_xq2[c] = einc("VQ")
                @block.tensor
                def _(te, w=w, c=c, v_xq2c=v_xq2[c]):
                    te.wait_ge(VQ, v_xq2c)
                    if c == 0:
                        if w > 0:
                            te.wait_ge(VQ, v_w2ev[(w - 1, 1)])
                    ins = None
                    for j in range(2):
                        lh = xq2[c % 2][:, 1 + 128 * j:1 + 128 * j + 128]
                        te.matmul(PS[2 + 2 * j][:, 0:512], lh, w2q8[c][:, 0:512],
                                  start=(c == 0), stop=(c == NCH - 1))
                        ins = te.matmul(PS[3 + 2 * j][:, 0:512], lh,
                                        w2q8[c][:, 512:1024],
                                        start=(c == 0), stop=(c == NCH - 1))
                    ins.then_inc(TQ, 1)
                t_w2[c] = einc("TQ")
            t_w2_done = t_w2[NCH - 1]

            @block.vector
            def _(v, v_gc=v_gc):
                v.wait_ge(DGC, v_gc)
                v.tensor_scalar(gcols[:, 0:2], gcols[:, 0:2], QCLIP, None,
                                ALU.max).then_inc(VQ, 1)
            v_gm = einc("VQ")
            @block.scalar
            def _(s, v_gm=v_gm):
                s.wait_ge(VQ, v_gm)
                s.activation(gcols[:, 0:2], gcols[:, 0:2], AF.Identity,
                             scale=cols[:, 24:25]).then_inc(SQs, 1)
            v_g = einc("SQ")

            for j in range(2):
                m = 2 * w + j
                @block.vector
                def _(v, w=w, j=j, m=m, t_w2_done=t_w2_done, v_g=v_g):
                    v.wait_ge(TQ, t_w2_done)
                    v.wait_ge(SQs, v_g)
                    v.scalar_tensor_tensor(y1f[j][:, 0:512], PS[2 + 2 * j][:, 0:512],
                                           gcols[:, j:j + 1], y1t[m][:, 0:512],
                                           ALU.mult, ALU.add,
                                           accum_out=cols[:, 10:11])
                    v.scalar_tensor_tensor(y1f[j][:, 512:1024],
                                           PS[3 + 2 * j][:, 0:512],
                                           gcols[:, j:j + 1], y1t[m][:, 512:1024],
                                           ALU.mult, ALU.add,
                                           accum_out=cols[:, 9:10]).then_inc(VQ, 1)
                v_y2 = einc("VQ")
                @block.scalar
                def _(s, j=j, v_y2=v_y2):
                    s.wait_ge(VQ, v_y2)
                    s.activation(scr[:, :], y1f[j][:, :], AF.Square,
                                 accum_out=cols[:, 11:12]).then_inc(SQs, 1)
                s_sq2 = einc("SQ")
                @block.gpsimd
                def _(g, s_sq2=s_sq2):
                    g.wait_ge(SQs, s_sq2)
                    g.tensor_tensor(cols[:, 12:13], cols[:, 10:11],
                                    cols[:, 9:10], ALU.add)
                    g.tensor_tensor(cols[:, 12:13], cols[:, 12:13],
                                    cc[:, 4:5], ALU.mult)
                    g.tensor_tensor(cols[:, 26:27], cols[:, 12:13],
                                    cols[:, 12:13], ALU.mult)
                    g.tensor_tensor(cols[:, 11:12], cols[:, 11:12],
                                    cc[:, 4:5], ALU.mult)
                    g.tensor_tensor(cols[:, 13:14], cols[:, 11:12],
                                    cols[:, 26:27], ALU.subtract)
                    g.tensor_tensor(cols[:, 13:14], cols[:, 13:14],
                                    cc[:, 5:6], ALU.add).then_inc(GQ, 1)
                g_l2 = einc("GQ")
                @block.scalar
                def _(s, g_l2=g_l2):
                    s.wait_ge(GQ, g_l2)
                    s.activation(cols[:, 27:28], cols[:, 13:14],
                                 AF.Sqrt).then_inc(SQs, 1)
                s_a2 = einc("SQ")
                @block.vector
                def _(v, s_a2=s_a2):
                    v.wait_ge(SQs, s_a2)
                    v.reciprocal(cols[:, 14:15],
                                 cols[:, 27:28]).then_inc(VQ, 1)
                v_a2 = einc("VQ")
                @block.gpsimd
                def _(g, v_a2=v_a2):
                    g.wait_ge(VQ, v_a2)
                    g.tensor_tensor(cols[:, 15:16], cols[:, 12:13],
                                    cols[:, 14:15], ALU.mult)
                    g.tensor_tensor(cols[:, 15:16], cols[:, 15:16],
                                    cols[:, 29:30], ALU.mult).then_inc(GQ, 1)
                g_b2 = einc("GQ")
                @block.scalar
                def _(s, w=w, j=j, g_b2=g_b2):
                    s.wait_ge(GQ, g_b2)
                    if len(g_store) >= 2:
                        s.wait_ge(DG, g_store[len(g_store) - 2])
                    s.activation(outt[j][:, :], y1f[j][:, :], AF.Identity,
                                 scale=cols[:, 14:15],
                                 bias=cols[:, 15:16]).then_inc(SQs, 1)
                s_out = einc("SQ")
                v_w2ev[(w, j)] = None
                @block.vector
                def _(v, s_out=s_out):
                    v.wait_ge(SQs, s_out)
                    v.memset(cols[0:1, 31:32], 0.0).then_inc(VQ, 1)
                v_w2ev[(w, j)] = einc("VQ")
                @block.gpsimd
                def _(g, m=m, j=j, s_out=s_out):
                    g.wait_ge(SQs, s_out)
                    g.dma_start(out=out_d[128 * m:128 * m + 128, :],
                                in_=outt[j][:, :]).then_inc(DG, 16)
                g_store.append(dinc("DG"))

        if stage == 5:
            @block.gpsimd
            def _(g):
                g.wait_ge(DG, g_store[-1])
                g.dma_start(out=dbg_d[:, 0:264],
                            in_=ybuf[:, 7960:8224]).then_inc(DG, 16)
                g.dma_start(out=dbg_d[:, 264:523],
                            in_=hbuf[:, 0:259]).then_inc(DG, 16)
                g.dma_start(out=dbg_d[:, 523:782],
                            in_=Bwin[:, 0:259]).then_inc(DG, 16)
                g.dma_start(out=dbg2_d[0:1, 0:269],
                            in_=arow_d[0:1, 1780:2049]).then_inc(DG, 16)
                dinc("DG"); dinc("DG"); dinc("DG"); dinc("DG")
                g.wait_ge(DG, cur("DG"))
                g.wait_ge(DL, cur("DL"))
                g.wait_ge(DGA, cur("DGA"))
                g.wait_ge(DGC, cur("DGC"))
                g.wait_ge(DSR, cur("DSR"))
            return nc

        @block.gpsimd
        def _(g):
            g.wait_ge(DG, g_store[-1])
            g.wait_ge(DL, cur("DL"))
            g.wait_ge(DGA, cur("DGA"))
            g.wait_ge(DGC, cur("DGC"))
            g.wait_ge(DSR, cur("DSR"))

    return nc


_PROG_CACHE = {}


def _get_prog(stage=4):
    if stage not in _PROG_CACHE:
        _PROG_CACHE[stage] = build_program(stage)
    return _PROG_CACHE[stage]


def kernel(L0_attn_query, L0_attn_Wq, L0_attn_Wk, L0_attn_Wv, L0_attn_Wo,
           L0_ffn_w1, L0_ffn_conv_w, L0_ffn_conv_b, L0_ffn_w2, _stage=4):
    x = np.ascontiguousarray(L0_attn_query)
    w1t = np.ascontiguousarray(np.asarray(L0_ffn_w1).T)
    w2t = np.ascontiguousarray(np.asarray(L0_ffn_w2).T)
    cw = np.ascontiguousarray(np.asarray(L0_ffn_conv_w).reshape(DHID, 3))
    cb = np.ascontiguousarray(L0_ffn_conv_b)
    wo = np.ascontiguousarray(L0_attn_Wo)
    wqa = np.ascontiguousarray(L0_attn_Wq)
    wka = np.ascontiguousarray(L0_attn_Wk)
    wva = np.ascontiguousarray(L0_attn_Wv)

    in_maps = []
    for c in range(8):
        b, qd = c // 4, c % 4
        lo = 2048 * qd - 1
        hi = 2048 * qd + 2049
        rows = np.arange(lo, hi)
        np.clip(rows, 0, 8191, out=rows)
        xs = np.ascontiguousarray(x[b][rows])
        mask = np.zeros((128, 3), np.float32)
        mask[:, 0] = 0.0 if qd == 0 else 1.0
        mask[:, 1] = 0.0 if qd == 3 else 1.0
        mask[:, 2] = 1.0
        mask[0, 2] = 0.0 if qd == 0 else 1.0
        mask[1, 2] = 0.0 if qd == 3 else 1.0
        in_maps.append({"x": xs, "wq": wqa, "wk": wka, "wv": wva, "wo": wo,
                        "w1t": w1t, "w2t": w2t, "cw": cw, "cb": cb,
                        "mask": mask})

    prog = _get_prog(_stage)
    res = run_bass_kernel_spmd(prog, in_maps, core_ids=list(range(8)))
    if _stage != 4:
        return res
    out = np.empty((2, 8192, 1024), np.float32)
    for c in range(8):
        b, qd = c // 4, c % 4
        out[b, 2048 * qd:2048 * qd + 2048, :] = res.results[c]["out"]
    return out


# version 07 (final)
# speedup vs baseline: 1.5908x; 1.5908x over previous
import contextlib
import numpy as np
import ml_dtypes
import concourse.bass as bass
from concourse.bass import broadcast_tensor_aps
import concourse.mybir as mybir
from concourse.bass_utils import run_bass_kernel_spmd

F32 = mybir.dt.float32
F16 = mybir.dt.float16
BF16 = mybir.dt.bfloat16
F8 = mybir.dt.float8e4
AF = mybir.ActivationFunctionType
ALU = mybir.AluOpType
AX = mybir.AxisListType.X

D = 1024
H = 16
NCP = 8
SQ = 2048
SH = 2050
SP = 2064
NT = 16
DHID = 4096
NCH = 32
WD = 256
NW = 8
C1 = float(np.float32(1.5 * 2 ** 23))
C2 = float(np.float32(1.5 * 2 ** 52))
EPS_ATTN = 1e-6
EPS_LN = 1e-5
QCLIP = 1e-5
INV_D = 1.0 / D
WNRM = 1.0 / (D * DHID)

_CN_DEBUG = None
KVZ_LIMIT = 99


def build_program(stage=4):
    nc = bass.Bass("TRN2", target_bir_lowering=False,
                   detect_race_conditions=False)

    x_d = nc.dram_tensor("x", [SH, D], F32, kind="ExternalInput").ap()
    wq_d = nc.dram_tensor("wq", [H, 64, 64], F32, kind="ExternalInput").ap()
    wk_d = nc.dram_tensor("wk", [H, 64, 64], F32, kind="ExternalInput").ap()
    wv_d = nc.dram_tensor("wv", [H, 64, 64], F32, kind="ExternalInput").ap()
    wo_d = nc.dram_tensor("wo", [D, D], F32, kind="ExternalInput").ap()
    w1t_d = nc.dram_tensor("w1t", [D, DHID], F32, kind="ExternalInput").ap()
    w2t_d = nc.dram_tensor("w2t", [DHID, D], F32, kind="ExternalInput").ap()
    cw_d = nc.dram_tensor("cw", [DHID, 3], F32, kind="ExternalInput").ap()
    cb_d = nc.dram_tensor("cb", [DHID], F32, kind="ExternalInput").ap()
    mask_d = nc.dram_tensor("mask", [128, 3], F32, kind="ExternalInput").ap()
    out_d = nc.dram_tensor("out", [SQ, D], F32, kind="ExternalOutput").ap()
    if stage != 4:
        dbg_d = nc.dram_tensor("dbg", [128, 1040], F32, kind="ExternalOutput").ap()
        dbg2_d = nc.dram_tensor("dbg2", [SP, D], F32, kind="ExternalOutput").ap()

    xbf_d = nc.dram_tensor("xbf_s", [SP, D], BF16, kind="Internal").ap()
    xq_dd = nc.dram_tensor("xq_s", [SP, D], BF16, kind="Internal").ap()
    ctx_dd = nc.dram_tensor("ctx_s", [SP, D], BF16, kind="Internal").ap()
    kvz_in = nc.dram_tensor("kvz_in", [128, 520], F32, kind="Internal").ap()
    kvz_out = nc.dram_tensor("kvz_out", [128, 520], F32, kind="Internal").ap()
    brow_d = nc.dram_tensor("brow_s", [1, SH], F32, kind="Internal").ap()
    arow_d = nc.dram_tensor("arow_s", [1, SH + 30], F32, kind="Internal").ap()
    arow_f = arow_d.rearrange("o s -> (o s)")

    identf_d = nc.inline_tensor(np.eye(128, dtype=np.float32), "identfsrc").ap()

    CN = {}

    def dinc(s):
        CN[s] = CN.get(s, 0) + 16
        return CN[s]

    def einc(s):
        CN[s] = CN.get(s, 0) + 1
        return CN[s]

    def cur(s):
        return CN.get(s, 0)

    global _CN_DEBUG
    _CN_DEBUG = CN
    with contextlib.ExitStack() as PERS:
        EP = PERS.enter_context

        def at(name, shape, dtype, off):
            return nc.alloc_sbuf_tensor_at(name, shape, dtype, offset=off + 2048)

        w1q8 = [at(f"w1q{j}", [128, DHID], F8, 16384 + 4096 * j) for j in range(8)]
        w2q8 = [at(f"w2q{c}", [128, D], F8, 49152 + 1024 * c) for c in range(NCH)]
        fmB = [at(f"fmB{j}", [128, SP], BF16, 81920 + 4128 * j) for j in range(8)]
        wo_bf = [at(f"wobf{j}", [128, D], BF16, 81920 + 2048 * j)
                 for j in range(8)]
        identf = at("identf", [128, 128], F32, 114944)
        maskc = at("maskc", [128, 3], F32, 115456)
        cc = at("cc", [128, 6], F32, 224160)
        cols = at("cols", [128, 32], F32, 115488)
        betac = at("betac", [128, 18], F32, 115616)
        wr = at("wr", [128, 2], F32, 115712)
        wsum2 = at("wsum2", [2, 1], F32, 115744)
        wrow2 = at("wrow2", [1, 2], F32, 115776)
        bhrow = at("bhrow", [1, 2], F32, 115808)
        monec = at("monec", [1, 128], F32, 115840)
        cwcol = at("cwcol", [128, 3 * NCH], F32, 116352)
        cbcol = at("cbcol", [128, NCH], F32, 116736)
        betacT = at("betacT", [16, 128], F32, 116864)
        bhalo = at("bhalo", [2, 1], F32, 117376)
        y1t = [at(f"y1t{k}", [128, D], F16, 117408 + 2048 * k) for k in range(NT)]
        y1h = at("y1h", [2, D], F16, 150176)
        kft = at("kft", [128, H * 64], BF16, 117408)
        v65 = at("v65", [128, H * 65], BF16, 119456)
        phe = at("phe", [128, 512], F32, 121568)
        phm = at("phm", [128, 512], F32, 123616)
        ctxt = [at(f"ctxt{i}", [128, D], BF16, 125664 + 2048 * i)
                for i in range(2)]
        qdg = [at(f"qdg{p}", [128, 128], BF16, 129760 + 256 * p)
               for p in range(NCP)]
        kdg = [at(f"kdg{p}", [128, 128], BF16, 131808 + 256 * p)
               for p in range(NCP)]
        vdg = [at(f"vdg{p}", [128, 128], BF16, 133856 + 256 * p)
               for p in range(NCP)]
        kvdg = [at(f"kvdg{p}", [128, 130], BF16, 135904 + 288 * p)
                for p in range(NCP)]
        wqkv_st = at("wqkv_st", [64, D], F32, 138208)
        kvzsb = at("kvzsb", [128, 520], F32, 142304)
        Rt = at("Rt", [128, 16], F32, 144416)
        y1f = [at(f"y1f{i}", [128, D], F32, 152224 + 4096 * i) for i in range(2)]
        outt = [at(f"outt{i}", [128, D], F32, 160416 + 4096 * i) for i in range(2)]
        scr = at("scr", [128, D], F32, 168608)
        wst = [at(f"wst{i}", [128, 512], F32, 152224 + 2048 * i) for i in range(2)]
        wqs = [at(f"wqs{i}", [128, 512], F32, 156320 + 2048 * i) for i in range(2)]
        fmA = [at(f"fmA{j}", [128, SP], BF16, 172704 + 4128 * j) for j in range(8)]
        ybuf = at("ybuf", [128, NCH * (WD + 1)], F32, 172704)
        xts = [at(f"xts{i}", [128, D], F32, 205728 + 4096 * i) for i in range(2)]
        xth = at("xth", [2, D], F32, 213920)
        castt = [at(f"castt{i}", [128, D], BF16, 218016 + 2048 * i)
                 for i in range(2)]
        xqt = castt
        zt14 = at("zt14", [16, D], BF16, 222112)
        Bwin = at("Bwin", [128, WD + 4], F32, 205728)
        S2t = at("S2t", [128, WD + 1], F32, 206784)
        Mx = at("Mx", [128, WD + 1], F32, 207840)
        Mn = at("Mn", [128, WD + 1], F32, 208896)
        hbuf = at("hbuf", [128, WD + 3], F32, 209952)
        hbuf2 = at("hbuf2", [128, WD + 3], F32, 224192)
        tc1 = at("tc1", [128, WD + 1], F32, 211008)
        tc2 = at("tc2", [128, WD + 1], F32, 212064)
        tc3 = at("tc3", [128, WD + 1], F32, 213120)
        pq = at("pq", [128, WD + 1], F32, 214176)
        pq2 = at("pq2", [128, WD + 1], F32, 225248)
        qr32 = at("qr32", [128, WD + 1], F32, 215232)
        qr64 = at("qr64", [128, WD + 1], F32, 216288)
        qwt = at("qwt", [128, WD + 1], F32, 217344)
        xq2 = [at(f"xq2_{i}", [128, WD + 1], BF16, 218400 + 544 * i)
               for i in range(2)]
        aseg = at("aseg", [1, 384], F32, 219488)
        s2row = at("s2row", [1, 384], F32, 221024)
        srow = at("srow", [1, 384], F32, 222560)
        amx = at("amx", [128, 4], F32, 224096)
        gcols = at("gcols", [128, 2], F32, 224128)

        PS = [EP(nc.psum_tensor(f"ps{i}", [128, 512], F32)) for i in range(8)]

        DL = EP(nc.semaphore("DL"))
        DG = EP(nc.semaphore("DG"))
        DX = [EP(nc.semaphore("DX0")), EP(nc.semaphore("DX1"))]
        DST = EP(nc.semaphore("DST"))
        DW = [EP(nc.semaphore("DW0")), EP(nc.semaphore("DW1"))]
        DR = [EP(nc.semaphore("DR0")), EP(nc.semaphore("DR1"))]
        DXH = EP(nc.semaphore("DXH"))
        DXQ = EP(nc.semaphore("DXQ"))
        DCX = EP(nc.semaphore("DCX"))
        DSR = EP(nc.semaphore("DSR"))
        DGC = EP(nc.semaphore("DGC"))
        DGA = EP(nc.semaphore("DGA"))
        DQKV = EP(nc.semaphore("DQKV"))
        BR = EP(nc.semaphore("BR"))
        TQ = EP(nc.semaphore("TQ"))
        VQ = EP(nc.semaphore("VQ"))
        SQs = EP(nc.semaphore("SQs"))
        GQ = EP(nc.semaphore("GQ"))
        CSs = EP(nc.semaphore("CSs"))
        block = EP(nc.Block())

        @block.sync
        def _(sy):
            sy.dma_start(out=identf[:, :], in_=identf_d).then_inc(DL, 16)
            sy.dma_start(out=maskc[:, :], in_=mask_d).then_inc(DL, 16)
            sy.dma_start(out=cwcol[:, :].rearrange("p (k t) -> p k t", t=3),
                         in_=cw_d.rearrange("(k p) t -> p k t", p=128)).then_inc(DL, 16)
            sy.dma_start(out=cbcol[:, :],
                         in_=cb_d.rearrange("(k p) -> p k", p=128),
                         allow_slow_non_contiguous=True).then_inc(DL, 16)
        for _i in range(4):
            dinc("DL")
        v_cast = []
        for t in range(17):
            @block.sync
            def _(sy, t=t):
                if t >= 2:
                    sy.wait_ge(VQ, v_cast[t - 2])
                if t < 16:
                    sy.dma_start(out=xts[t % 2][:, :],
                                 in_=x_d[128 * t:128 * t + 128, :]).then_inc(DX[t % 2], 16)
                else:
                    sy.dma_start(out=xth[:, :], in_=x_d[2048:2050, :]).then_inc(DX[t % 2], 16)
            v_xl = dinc(f"DX{t % 2}")
            @block.vector
            def _(v, t=t, v_xl=v_xl):
                v.wait_ge(DX[t % 2], v_xl)
                if t >= 2:
                    v.wait_ge(DST, 16 * (t - 1))
                p = 128 if t < 16 else 2
                src = xts[t % 2][:, :] if t < 16 else xth[:, :]
                v.tensor_copy(castt[t % 2][0:p, :], src).then_inc(VQ, 1)
            v_cast.append(einc("VQ"))
            @block.sync
            def _(sy, t=t):
                sy.wait_ge(VQ, v_cast[t])
                p = 128 if t < 16 else 2
                sy.dma_start(out=xbf_d[128 * t:128 * t + p, :],
                             in_=castt[t % 2][0:p, :]).then_inc(DST, 16)
            dinc("DST")

        @block.vector
        def _(v):
            v.memset(cols[:, 0:1], 1.0)
            v.memset(cols[:, 29:30], -1.0)
            v.memset(monec[:, :], 1.0)
            v.memset(wr[:, 0:2], 0.0)
            v.memset(cc[:, 0:1], 0.5)
            v.memset(cc[:, 1:2], -0.5)
            v.memset(cc[:, 2:3], 127.0)
            v.memset(cc[:, 3:4], 1.0 / 127.0)
            v.memset(cc[:, 4:5], INV_D)
            v.memset(cc[:, 5:6], EPS_LN)
            v.memset(zt14[:, :], 0.0)
            v.memset(v65[:, :], 0.0)
            ins = None
            for hh in range(H):
                ins = v.memset(v65[:, 65 * hh + 64:65 * hh + 65], 1.0)
            ins = v.tensor_scalar(cols[:, 1:2], cols[:, 0:1], C2, C2,
                                  ALU.add, ALU.subtract)
            ins.then_inc(VQ, 1)
        v_const = einc("VQ")

        @block.sync
        def _(sy):
            sy.wait_ge(VQ, v_const)
            sy.dma_start(out=xbf_d[2050:2064, :], in_=zt14[0:14, :]).then_inc(DL, 16)
        v_xpad = dinc("DL")
        v_dst_all = cur("DST")

        v_dg = []
        for proj, (wsrc, dgt) in enumerate(((wq_d, qdg), (wk_d, kdg), (wv_d, vdg))):
            @block.sync
            def _(sy, wsrc=wsrc, proj=proj):
                if proj > 0:
                    sy.wait_ge(VQ, v_dg[-1])
                sy.dma_start(out=wqkv_st[:, :].rearrange("d (h e) -> d h e", e=64),
                             in_=wsrc.rearrange("h d e -> d h e")).then_inc(DQKV, 16)
            v_w = dinc("DQKV")
            @block.vector
            def _(v, dgt=dgt, v_w=v_w):
                v.wait_ge(DQKV, v_w)
                ins = None
                for cp in range(NCP):
                    v.memset(dgt[cp][:, :], 0.0)
                    v.tensor_copy(dgt[cp][0:64, 0:64],
                                  wqkv_st[:, 128 * cp:128 * cp + 64])
                    ins = v.tensor_copy(dgt[cp][64:128, 64:128],
                                        wqkv_st[:, 128 * cp + 64:128 * cp + 128])
                ins.then_inc(VQ, 1)
            v_dg.append(einc("VQ"))
        v_dg_done = v_dg[-1]

        @block.sync
        def _(sy):
            sy.wait_ge(DL, v_xpad)
            sy.wait_ge(DST, v_dst_all)
            for j in range(8):
                sy.dma_start_transpose(
                    fmA[j][:, :], xbf_d[:, 128 * j:128 * j + 128]).then_inc(DL, 16)
        for j in range(8):
            dinc("DL")
        @block.sync
        def _(sy, tot=cur("DL")):
            sy.wait_ge(DL, tot)
            sy.sem_inc(BR, 1)
        br1 = einc("BR")

        pw = {"red": [], "ga": [], "wq": [], "mcols": None}

        def _pwA(wi, pc):
            def go():
                k = len(pw["red"])
                @block.sync
                def _(sy, wi=wi, pc=pc, k=k):
                    if k >= 2:
                        sy.wait_ge(VQ, pw["red"][k - 2])
                    if wi == 0:
                        s_ = w1t_d[128 * (pc // 8):128 * (pc // 8) + 128,
                                   512 * (pc % 8):512 * (pc % 8) + 512]
                    else:
                        s_ = w2t_d[128 * (pc // 2):128 * (pc // 2) + 128,
                                   512 * (pc % 2):512 * (pc % 2) + 512]
                    sy.dma_start(out=wst[k % 2][:, :], in_=s_).then_inc(DW[k % 2], 16)
                v_wl = dinc(f"DW{k % 2}")
                csl = (slice(9, 10) if k % 2 == 0 else slice(30, 31))
                @block.vector
                def _(v, wi=wi, k=k, v_wl=v_wl, csl=csl,
                      gprev=(pw["ga"][k - 2] if k >= 2 else None)):
                    v.wait_ge(DW[k % 2], v_wl)
                    if gprev is not None:
                        v.wait_ge(GQ, gprev)
                    v.tensor_reduce(cols[:, csl], wst[k % 2][:, :], AX, ALU.add,
                                    apply_absolute_value=True).then_inc(VQ, 1)
                pw["red"].append(einc("VQ"))
                @block.gpsimd
                def _(g, wi=wi, csl=csl, vr=pw["red"][-1]):
                    g.wait_ge(VQ, vr)
                    g.tensor_tensor(wr[:, wi:wi + 1], wr[:, wi:wi + 1],
                                    cols[:, csl], ALU.add).then_inc(GQ, 1)
                pw["ga"].append(einc("GQ"))
            return go

        def _pw_mcols():
            @block.tensor
            def _(te):
                te.wait_ge(GQ, pw["ga"][-1])
                te.wait_ge(VQ, pw["qf_done"])
                te.transpose(PS[0][0:2, 0:128], wr[:, 0:2],
                             identf[:, :]).then_inc(TQ, 1)
            t_wrT = einc("TQ")
            @block.vector
            def _(v, t_wrT=t_wrT):
                v.wait_ge(TQ, t_wrT)
                v.reduce_sum(wsum2[0:2, 0:1], PS[0][0:2, 0:128],
                             AX).then_inc(VQ, 1)
            v_ws = einc("VQ")
            @block.tensor
            def _(te, v_ws=v_ws):
                te.wait_ge(VQ, v_ws)
                te.transpose(PS[0][0:1, 0:2], wsum2[0:2, 0:1],
                             identf[0:2, 0:2]).then_inc(TQ, 1)
            t_wsT = einc("TQ")
            @block.vector
            def _(v, t_wsT=t_wsT):
                v.wait_ge(TQ, t_wsT)
                v.tensor_copy(wrow2[0:1, 0:2], PS[0][0:1, 0:2]).then_inc(VQ, 1)
            v_wr2 = einc("VQ")
            @block.tensor
            def _(te, v_wr2=v_wr2):
                te.wait_ge(VQ, v_wr2)
                te.matmul(PS[0][:, 0:2], monec[0:1, :], wrow2[0:1, 0:2],
                          start=True, stop=True).then_inc(TQ, 1)
            t_wbc = einc("TQ")
            @block.vector
            def _(v, t_wbc=t_wbc):
                v.wait_ge(TQ, t_wbc)
                v.tensor_scalar(cols[:, 2:4], PS[0][:, 0:2], WNRM, QCLIP,
                                ALU.mult, ALU.max).then_inc(VQ, 1)
            v_m12 = einc("VQ")
            @block.scalar
            def _(s, v_m12=v_m12):
                s.wait_ge(VQ, v_m12)
                s.activation(cols[:, 4:5], cols[:, 2:3], AF.Identity,
                             scale=cc[:, 0:1])
                s.activation(cols[:, 5:6], cols[:, 2:3], AF.Identity,
                             scale=cc[:, 1:2])
                s.activation(cols[:, 6:7], cols[:, 3:4], AF.Identity,
                             scale=cc[:, 0:1])
                s.activation(cols[:, 7:8], cols[:, 3:4], AF.Identity,
                             scale=cc[:, 1:2])
                s.activation(cols[:, 24:25], cols[:, 3:4], AF.Identity,
                             scale=cc[:, 3:4])
                s.activation(cols[:, 8:9], cols[:, 2:3], AF.Identity,
                             scale=cc[:, 3:4]).then_inc(SQs, 1)
            pw["mcols"] = einc("SQ")

        def _pwB(wi, pc):
            def go():
                k = len(pw["wq"])
                @block.sync
                def _(sy, wi=wi, pc=pc, k=k):
                    if k >= 2:
                        sy.wait_ge(VQ, pw["wq"][k - 2])
                    else:
                        sy.wait_ge(VQ, pw["red"][-1])
                    if wi == 0:
                        s_ = w1t_d[128 * (pc // 8):128 * (pc // 8) + 128,
                                   512 * (pc % 8):512 * (pc % 8) + 512]
                    else:
                        s_ = w2t_d[128 * (pc // 2):128 * (pc // 2) + 128,
                                   512 * (pc % 2):512 * (pc % 2) + 512]
                    sy.dma_start(out=wst[k % 2][:, :], in_=s_).then_inc(DW[k % 2], 16)
                v_wl = dinc(f"DW{k % 2}")
                if wi == 0:
                    dst = w1q8[pc // 8][:, 512 * (pc % 8):512 * (pc % 8) + 512]
                    @block.scalar
                    def _(s, pc=pc, k=k, v_wl=v_wl):
                        s.wait_ge(DW[k % 2], v_wl)
                        s.wait_ge(SQs, pw["mcols"])
                        if pw["wq"]:
                            s.wait_ge(VQ, pw["wq"][-1])
                        s.activation(wqs[0][:, :], wst[k % 2][:, :],
                                     AF.Sign, bias=cols[:, 5:6])
                        s.activation(wqs[1][:, :], wst[k % 2][:, :],
                                     AF.Sign, bias=cols[:, 4:5]).then_inc(SQs, 1)
                    s_w = einc("SQ")
                    @block.vector
                    def _(v, dst=dst, s_w=s_w):
                        v.wait_ge(SQs, s_w)
                        v.tensor_tensor(wqs[0][:, :], wqs[0][:, :], wqs[1][:, :],
                                        ALU.add)
                        v.tensor_scalar(dst, wqs[0][:, :], 0.5, None,
                                        ALU.mult).then_inc(VQ, 1)
                    pw["wq"].append(einc("VQ"))
                else:
                    dst = w2q8[pc // 2][:, 512 * (pc % 2):512 * (pc % 2) + 512]
                    @block.vector
                    def _(v, dst=dst, k=k, v_wl=v_wl):
                        v.wait_ge(DW[k % 2], v_wl)
                        v.wait_ge(SQs, pw["mcols"])
                        v.tensor_scalar(wqs[0][:, :], wst[k % 2][:, :],
                                        cols[:, 6:7], None, ALU.is_gt)
                        v.tensor_scalar(wqs[1][:, :], wst[k % 2][:, :],
                                        cols[:, 7:8], None, ALU.is_lt)
                        v.tensor_tensor(dst, wqs[0][:, :], wqs[1][:, :],
                                        ALU.subtract).then_inc(VQ, 1)
                    pw["wq"].append(einc("VQ"))
            return go

        pw_steps = []
        for wi in range(2):
            for pc in range(64):
                pw_steps.append(_pwA(wi, pc))
        pw_steps.append(_pw_mcols)
        for wi in range(2):
            for pc in range(64):
                pw_steps.append(_pwB(wi, pc))

        def drain_pw(n):
            for _ in range(n):
                if pw_steps:
                    pw_steps.pop(0)()

        qchunks = [(0, 512), (512, 512), (1024, 512), (1536, 512), (2048, 2)]
        bank_free_q = [None, None]
        v_qf_done = None
        for cp in range(NCP):
            for ci, (c0, cn) in enumerate(qchunks):
                idx = cp * len(qchunks) + ci
                bk = idx % 2
                need = bank_free_q[bk]
                @block.tensor
                def _(te, cp=cp, c0=c0, cn=cn, bk=bk, idx=idx, need=need):
                    if idx == 0:
                        te.wait_ge(BR, br1)
                        te.wait_ge(VQ, v_dg[0])
                    if need is not None:
                        te.wait_ge(VQ, need)
                    te.matmul(PS[bk][:, 0:cn], qdg[cp][:, :],
                              fmA[cp][:, c0:c0 + cn],
                              start=True, stop=True).then_inc(TQ, 1)
                t_q = einc("TQ")
                prev_stt = v_qf_done
                @block.scalar
                def _(s, t_q=t_q, cn=cn, bk=bk, prev_stt=prev_stt):
                    s.wait_ge(TQ, t_q)
                    if prev_stt is not None:
                        s.wait_ge(VQ, prev_stt)
                    s.activation(phm[:, 0:cn], PS[bk][:, 0:cn], AF.Relu,
                                 scale=cols[:, 29:30])
                    s.activation(phe[:, 0:cn], phm[:, 0:cn], AF.Exp,
                                 scale=cols[:, 29:30]).then_inc(SQs, 1)
                s_q = einc("SQ")
                @block.vector
                def _(v, s_q=s_q, cn=cn, bk=bk, cp=cp, c0=c0):
                    v.wait_ge(SQs, s_q)
                    v.scalar_tensor_tensor(fmB[cp][:, c0:c0 + cn], PS[bk][:, 0:cn],
                                           0.0, phe[:, 0:cn],
                                           ALU.max, ALU.add).then_inc(VQ, 1)
                v_qf_done = bank_free_q[bk] = einc("VQ")
                drain_pw(2)
        pw["qf_done"] = v_qf_done

        v_kvevac = []
        t_kvz_list = []
        for t in range(17):
            p = 128 if t < 16 else 2
            @block.tensor
            def _(te, t=t, p=p):
                if t == 0:
                    te.wait_ge(VQ, v_dg_done)
                    te.wait_ge(BR, br1)
                if t >= 1:
                    te.wait_ge(VQ, v_kvevac[t - 1])
                ins = None
                for cp in range(NCP):
                    kb = PS[2 + cp // 4]
                    vb = PS[4 + cp // 4]
                    off = 128 * (cp % 4)
                    te.matmul(kb[0:p, off:off + 128],
                              fmA[cp][:, 128 * t:128 * t + p],
                              kdg[cp][:, :], start=True, stop=True)
                    ins = te.matmul(vb[0:p, off:off + 128],
                                    fmA[cp][:, 128 * t:128 * t + p],
                                    vdg[cp][:, :], start=True, stop=True)
                ins.then_inc(TQ, 1)
            tt_kv = einc("TQ")
            if t >= 1 and t - 1 < KVZ_LIMIT:
                tp_prev = t - 1
                @block.tensor
                def _(te, tp=tp_prev):
                    te.wait_ge(VQ, v_kvevac[tp])
                    pk = 1 if tp == 16 else (128 if tp < 16 else 2)
                    ins = None
                    for hh in range(H):
                        cp = hh // 2
                        bank = PS[6 + cp // 4]
                        colo = 65 * (cp % 4)
                        tpos = (0, 64) if hh % 2 == 1 else None
                        po = 64 * (hh % 2)
                        ins = te.matmul(bank[po:po + 64, colo:colo + 65],
                                        kft[0:pk, 64 * hh:64 * hh + 64],
                                        v65[0:pk, 65 * hh:65 * hh + 65],
                                        start=(tp == 0 and hh in (0, 1, 8, 9)),
                                        stop=(tp == KVZ_LIMIT - 1 and hh in (6, 7, 14, 15)),
                                        skip_group_check=True,
                                        tile_position=tpos)
                    ins.then_inc(TQ, 1)
                t_kvz_list.append(einc("TQ"))
            gate = cur("TQ")
            prev_kstt = v_kvevac[t - 1] if t >= 1 else v_qf_done
            s_kf = []
            v_kstt = []
            for half in range(2):
                @block.scalar
                def _(s, gate=gate, p=p, half=half, prev=prev_kstt,
                      pv=(v_kstt[0] if half == 1 else None)):
                    s.wait_ge(TQ, gate)
                    if half == 0 and prev is not None:
                        s.wait_ge(VQ, prev)
                    if half == 1:
                        s.wait_ge(VQ, pv)
                    s.activation(phm[0:p, 0:512], PS[2 + half][0:p, 0:512],
                                 AF.Relu, scale=cols[0:p, 29:30])
                    s.activation(phe[0:p, 0:512], phm[0:p, 0:512],
                                 AF.Exp, scale=cols[0:p, 29:30]).then_inc(SQs, 1)
                s_kf.append(einc("SQ"))
                @block.vector
                def _(v, p=p, half=half, s_h=s_kf[half]):
                    v.wait_ge(SQs, s_h)
                    v.scalar_tensor_tensor(
                        kft[0:p, 512 * half:512 * half + 512],
                        PS[2 + half][0:p, 0:512], 0.0,
                        phe[0:p, 0:512], ALU.max, ALU.add).then_inc(VQ, 1)
                v_kstt.append(einc("VQ"))
            @block.vector
            def _(v, gate=gate, p=p, t=t):
                v.wait_ge(TQ, gate)
                ins = None
                for half in range(2):
                    for cp4 in range(4):
                        cp = 4 * half + cp4
                        for hh in range(2):
                            h_abs = 2 * cp + hh
                            ins = v.tensor_copy(
                                v65[0:p, 65 * h_abs:65 * h_abs + 64],
                                PS[4 + half][0:p,
                                             128 * cp4 + 64 * hh:128 * cp4 + 64 * hh + 64])
                if t == 0:
                    ins = v.memset(kft[0:1, :], 0.0)
                ins.then_inc(VQ, 1)
            v_kvevac.append(einc("VQ"))
            drain_pw(4)
        @block.tensor
        def _(te):
            if KVZ_LIMIT <= 16:
                te.engine_nop() if False else None
            te.wait_ge(VQ, v_kvevac[16])
            ins = None
            ins = None
            if KVZ_LIMIT > 16:
                for hh in range(H):
                    cp = hh // 2
                    bank = PS[6 + cp // 4]
                    colo = 65 * (cp % 4)
                    tpos = (0, 64) if hh % 2 == 1 else None
                    po = 64 * (hh % 2)
                    ins = te.matmul(bank[po:po + 64, colo:colo + 65],
                                    kft[0:1, 64 * hh:64 * hh + 64],
                                    v65[0:1, 65 * hh:65 * hh + 65],
                                    start=False, stop=(hh in (6, 7, 14, 15)),
                                    skip_group_check=True, tile_position=tpos)
                ins.then_inc(TQ, 1)
            else:
                ins = te.memset(cols[0:1, 31:32], 0.0) if False else None
                te.wait_ge(TQ, cur("TQ"))
                import concourse.mybir as _mb
                ins = te.matmul(PS[4][0:1, 0:1], cols[0:1, 0:1], cols[0:1, 0:1],
                                start=True, stop=True)
                ins.then_inc(TQ, 1)
        t_kvz_done = einc("TQ")

        @block.vector
        def _(v):
            v.wait_ge(TQ, t_kvz_done)
            v.tensor_copy(kvzsb[:, 0:260], PS[6][:, 0:260])
            v.tensor_copy(kvzsb[:, 260:520], PS[7][:, 0:260]).then_inc(VQ, 1)
        v_kvzevac = einc("VQ")
        @block.gpsimd
        def _(g):
            g.wait_ge(VQ, v_kvzevac)
            g.dma_start(out=kvz_in, in_=kvzsb[:, :]).then_inc(DG, 16)
        g_kvzout = dinc("DG")
        @block.gpsimd
        def _(g):
            g.wait_ge(DG, g_kvzout)
            g.collective_compute(
                "AllReduce", ALU.add, ins=[kvz_in], outs=[kvz_out],
                replica_groups=[[0, 1, 2, 3], [4, 5, 6, 7]]).then_inc(CSs, 1)
        c_ar = einc("CS")
        @block.gpsimd
        def _(g):
            g.wait_ge(CSs, c_ar)
            g.dma_start(out=kvzsb[:, :], in_=kvz_out).then_inc(DG, 16)
        g_kvzin = dinc("DG")
        @block.vector
        def _(v):
            v.wait_ge(DG, g_kvzin)
            ins = None
            for cp in range(NCP):
                c0 = 65 * cp
                v.memset(kvdg[cp][:, :], 0.0)
                v.tensor_copy(kvdg[cp][0:64, 0:64], kvzsb[0:64, c0:c0 + 64])
                v.tensor_copy(kvdg[cp][64:128, 64:128], kvzsb[64:128, c0:c0 + 64])
                v.tensor_copy(kvdg[cp][0:64, 128:129], kvzsb[0:64, c0 + 64:c0 + 65])
                ins = v.tensor_copy(kvdg[cp][64:128, 129:130],
                                    kvzsb[64:128, c0 + 64:c0 + 65])
            ins.then_inc(VQ, 1)
        v_kvdg = einc("VQ")

        if stage == 1:
            @block.gpsimd
            def _(g):
                g.wait_ge(VQ, v_kvdg)
                g.dma_start(out=dbg_d[:, 0:520], in_=kvzsb[:, :]).then_inc(DG, 16)
                g.dma_start(out=dbg2_d, in_=xbf_d).then_inc(DG, 16)
                dinc("DG"); dinc("DG")
                g.wait_ge(DG, cur("DG"))
                g.wait_ge(DL, cur("DL"))
            return nc

        v_ctxstage = []
        v_blk = {}
        t_num_ev = {}
        units_p4 = [(tt, cc_) for tt in range(17) for cc_ in range(NCP)]
        NIDX = len(units_p4)
        for bi in range(NIDX + 2):
            if bi < NIDX:
                t, cp = units_p4[bi]
                p = 128 if t < 16 else 2
                @block.tensor
                def _(te, bi=bi, t=t, cp=cp, p=p):
                    if bi == 0:
                        te.wait_ge(VQ, v_kvdg)
                        te.wait_ge(VQ, v_qf_done)
                        te.wait_ge(SQs, pw["mcols"])
                    if bi >= 4:
                        te.wait_ge(VQ, v_blk[bi - 2])
                    te.matmul(PS[bi % 4][0:p, 0:130],
                              fmB[cp][:, 128 * t:128 * t + p],
                              kvdg[cp][:, :], start=True, stop=True).then_inc(TQ, 1)
                t_num_ev[bi] = einc("TQ")
            @block.vector
            def _(v, bi=bi):
                ins = None
                if bi < NIDX:
                    v.wait_ge(TQ, t_num_ev[bi])
                    t, cp = units_p4[bi]
                    p = 128 if t < 16 else 2
                    sl = 4 * (bi % 2)
                    ins = v.tensor_scalar(Rt[0:p, sl:sl + 2],
                                          PS[bi % 4][0:p, 128:130],
                                          EPS_ATTN, None, ALU.add)
                if 0 <= bi - 1 < NIDX:
                    t, cp = units_p4[bi - 1]
                    p = 128 if t < 16 else 2
                    sl = 4 * ((bi - 1) % 2)
                    ins = v.reciprocal(Rt[0:p, sl + 2:sl + 4],
                                       Rt[0:p, sl:sl + 2])
                if bi - 2 >= 0:
                    t, cp = units_p4[bi - 2]
                    p = 128 if t < 16 else 2
                    sl = 4 * ((bi - 2) % 2)
                    if cp == 0 and t >= 2:
                        v.wait_ge(DCX, 16 * (t - 1))
                    for hh in range(2):
                        ins = v.tensor_scalar(
                            ctxt[t % 2][0:p,
                                        128 * cp + 64 * hh:128 * cp + 64 * hh + 64],
                            PS[(bi - 2) % 4][0:p, 64 * hh:64 * hh + 64],
                            Rt[0:p, sl + 2 + hh:sl + 3 + hh], None, ALU.mult)
                ins.then_inc(VQ, 1)
            v_blk[bi] = einc("VQ")
            if bi >= 9 and (bi - 9) % 8 == 0:
                tso = (bi - 9) // 8
                pso = 2 if tso == 16 else 128
                @block.sync
                def _(sy, tso=tso, pso=pso, vb=v_blk[bi]):
                    sy.wait_ge(VQ, vb)
                    sy.dma_start(out=ctx_dd[128 * tso:128 * tso + pso, :],
                                 in_=ctxt[tso % 2][0:pso, :]).then_inc(DCX, 16)
                v_ctxstage.append(dinc("DCX"))
            drain_pw(1)
        drain_pw(len(pw_steps))
        v_wq_done = pw["wq"][-1]
        @block.sync
        def _(sy):
            sy.dma_start(out=ctx_dd[2050:2064, :], in_=zt14[0:14, :]).then_inc(DL, 16)
        v_ctxpad = dinc("DL")
        @block.sync
        def _(sy, dcx_all=cur("DCX")):
            sy.wait_ge(DL, v_ctxpad)
            sy.wait_ge(DCX, dcx_all)
            for j in range(8):
                sy.dma_start_transpose(
                    fmA[j][:, :], ctx_dd[:, 128 * j:128 * j + 128]).then_inc(DL, 16)
        for j in range(8):
            dinc("DL")
        @block.sync
        def _(sy, tot=cur("DL")):
            sy.wait_ge(DL, tot)
            sy.sem_inc(BR, 1)
        br2 = einc("BR")

        if stage == 2:
            @block.gpsimd
            def _(g):
                g.wait_ge(BR, br2)
                g.dma_start(out=dbg2_d, in_=ctx_dd).then_inc(DG, 16)
                g.dma_start(out=dbg_d[:, 0:520], in_=kvzsb[:, :]).then_inc(DG, 16)
                dinc("DG"); dinc("DG")
                g.wait_ge(DG, cur("DG"))
                g.wait_ge(DL, cur("DL"))
            return nc

        v_wocast = []
        for j in range(8):
            @block.sync
            def _(sy, j=j):
                if j >= 2:
                    sy.wait_ge(VQ, v_wocast[j - 2])
                sy.dma_start(out=xts[j % 2][:, :],
                             in_=wo_d[128 * j:128 * j + 128, :]).then_inc(DR[j % 2], 16)
            v_wl = dinc(f"DR{j % 2}")
            @block.vector
            def _(v, j=j, v_wl=v_wl):
                v.wait_ge(DR[j % 2], v_wl)
                if j == 7:
                    v.wait_ge(BR, br2)
                v.tensor_copy(wo_bf[j][:, :], xts[j % 2][:, :]).then_inc(VQ, 1)
            v_wocast.append(einc("VQ"))
        v_wo_done = v_wocast[-1]

        units = [(k, 128) for k in range(NT)] + [(16, 2)]
        v_ln1_ev = {}
        v_xqstage = []
        for ui, (k, p) in enumerate(units):
            bko = 2 * (ui % 2)
            need = v_ln1_ev.get(ui - 2)
            @block.sync
            def _(sy, ui=ui, k=k, p=p):
                if ui == 0:
                    sy.wait_ge(VQ, v_wo_done)
                if ui >= 2:
                    sy.wait_ge(VQ, v_ln1_ev[ui - 2])
                if p == 128:
                    sy.dma_start(out=xts[ui % 2][:, :],
                                 in_=x_d[1 + 128 * k:1 + 128 * k + 128, :]).then_inc(DR[ui % 2], 16)
                else:
                    sy.dma_start(out=xth[:, :],
                                 in_=x_d[0:SH:2049, :]).then_inc(DXH, 16)
            v_xr = dinc(f"DR{ui % 2}") if p == 128 else dinc("DXH")
            @block.tensor
            def _(te, ui=ui, k=k, p=p, bko=bko, need=need):
                if ui == 0:
                    te.wait_ge(BR, br2)
                    te.wait_ge(VQ, v_wo_done)
                if need is not None:
                    te.wait_ge(VQ, need)
                ins = None
                for e in range(8):
                    if p == 128:
                        lh = fmA[e][:, 1 + 128 * k:1 + 128 * k + 128]
                    else:
                        lh = fmA[e][:, 0:SH:2049]
                    te.matmul(PS[bko][0:p, 0:512], lh, wo_bf[e][:, 0:512],
                              start=(e == 0), stop=(e == 7))
                    ins = te.matmul(PS[bko + 1][0:p, 0:512], lh, wo_bf[e][:, 512:1024],
                                    start=(e == 0), stop=(e == 7))
                ins.then_inc(TQ, 1)
            t_wo = einc("TQ")
            @block.vector
            def _(v, ui=ui, k=k, p=p, bko=bko, t_wo=t_wo, v_xr=v_xr):
                v.wait_ge(TQ, t_wo)
                v.wait_ge(DR[ui % 2] if p == 128 else DXH, v_xr)
                xsrc = xts[ui % 2] if p == 128 else xth
                v.scalar_tensor_tensor(y1f[ui % 2][0:p, 0:512], PS[bko][0:p, 0:512],
                                       1.0, xsrc[0:p, 0:512], ALU.mult, ALU.add,
                                       accum_out=cols[0:p, 10:11])
                v.scalar_tensor_tensor(y1f[ui % 2][0:p, 512:1024],
                                       PS[bko + 1][0:p, 0:512],
                                       1.0, xsrc[0:p, 512:1024], ALU.mult, ALU.add,
                                       accum_out=cols[0:p, 9:10])
                v.tensor_reduce(cols[0:p, 16:17], y1f[ui % 2][0:p, :], AX, ALU.max)
                v.tensor_reduce(cols[0:p, 17:18], y1f[ui % 2][0:p, :], AX,
                                ALU.min).then_inc(VQ, 1)
            v_res = einc("VQ")
            @block.scalar
            def _(s, ui=ui, p=p, v_res=v_res):
                s.wait_ge(VQ, v_res)
                s.activation(outt[(ui + 1) % 2][0:p, :], y1f[ui % 2][0:p, :],
                             AF.Square,
                             accum_out=cols[0:p, 11:12]).then_inc(SQs, 1)
            s_sq = einc("SQ")
            @block.gpsimd
            def _(g, p=p, s_sq=s_sq):
                g.wait_ge(SQs, s_sq)
                g.tensor_tensor(cols[0:p, 12:13], cols[0:p, 10:11],
                                cols[0:p, 9:10], ALU.add)
                g.tensor_tensor(cols[0:p, 12:13], cols[0:p, 12:13],
                                cc[0:p, 4:5], ALU.mult)
                g.tensor_tensor(cols[0:p, 26:27], cols[0:p, 12:13],
                                cols[0:p, 12:13], ALU.mult)
                g.tensor_tensor(cols[0:p, 11:12], cols[0:p, 11:12],
                                cc[0:p, 4:5], ALU.mult)
                g.tensor_tensor(cols[0:p, 13:14], cols[0:p, 11:12],
                                cols[0:p, 26:27], ALU.subtract)
                g.tensor_tensor(cols[0:p, 13:14], cols[0:p, 13:14],
                                cc[0:p, 5:6], ALU.add)
                g.tensor_tensor(cols[0:p, 18:19], cols[0:p, 16:17],
                                cols[0:p, 12:13], ALU.subtract)
                g.tensor_tensor(cols[0:p, 19:20], cols[0:p, 12:13],
                                cols[0:p, 17:18], ALU.subtract).then_inc(GQ, 1)
            g_st = einc("GQ")
            @block.scalar
            def _(s, p=p, g_st=g_st):
                s.wait_ge(GQ, g_st)
                s.activation(cols[0:p, 27:28], cols[0:p, 13:14],
                             AF.Sqrt).then_inc(SQs, 1)
            s_a = einc("SQ")
            @block.vector
            def _(v, p=p, s_a=s_a):
                v.wait_ge(SQs, s_a)
                v.tensor_tensor(cols[0:p, 20:21], cols[0:p, 18:19],
                                cols[0:p, 19:20], ALU.max)
                v.reciprocal(cols[0:p, 14:15], cols[0:p, 27:28]).then_inc(VQ, 1)
            v_a = einc("VQ")
            @block.gpsimd
            def _(g, p=p, v_a=v_a):
                g.wait_ge(VQ, v_a)
                g.tensor_tensor(cols[0:p, 21:22], cols[0:p, 20:21],
                                cols[0:p, 14:15], ALU.mult).then_inc(GQ, 1)
            g_am = einc("GQ")
            @block.vector
            def _(v, ui=ui, k=k, p=p, g_am=g_am):
                v.wait_ge(GQ, g_am)
                v.tensor_scalar(cols[0:p, 21:22], cols[0:p, 21:22], QCLIP, None,
                                ALU.max)
                dst = y1t[k][0:p, :] if p == 128 else y1h[0:p, :]
                v.tensor_copy(dst, y1f[ui % 2][0:p, :])
                v.reciprocal(cols[0:p, 25:26], cols[0:p, 21:22]).then_inc(VQ, 1)
            v_ir = einc("VQ")
            @block.gpsimd
            def _(g, ui=ui, k=k, p=p, v_ir=v_ir):
                g.wait_ge(VQ, v_ir)
                g.tensor_tensor(cols[0:p, 28:29], cols[0:p, 25:26],
                                cc[0:p, 2:3], ALU.mult)
                g.tensor_tensor(cols[0:p, 22:23], cols[0:p, 14:15],
                                cols[0:p, 28:29], ALU.mult)
                g.tensor_tensor(cols[0:p, 15:16], cols[0:p, 12:13],
                                cols[0:p, 14:15], ALU.mult)
                g.tensor_tensor(cols[0:p, 15:16], cols[0:p, 15:16],
                                cols[0:p, 29:30], ALU.mult)
                g.tensor_tensor(cols[0:p, 23:24], cols[0:p, 15:16],
                                cols[0:p, 28:29], ALU.mult)
                bc = betac[0:p, k:k + 1] if p == 128 else bhalo[0:2, 0:1]
                ins = g.tensor_tensor(bc, cols[0:p, 21:22], cols[0:p, 8:9],
                                      ALU.mult)
                if p == 2:
                    ins = g.tensor_tensor(bc, bc, maskc[0:2, 2:3], ALU.mult)
                ins.then_inc(GQ, 1)
            g_ab = einc("GQ")
            @block.vector
            def _(v, ui=ui, p=p, g_ab=g_ab):
                v.wait_ge(GQ, g_ab)
                v.tensor_scalar(scr[0:p, :], y1f[ui % 2][0:p, :],
                                cols[0:p, 22:23], cols[0:p, 23:24],
                                ALU.mult, ALU.add)
                v.tensor_scalar(outt[(ui + 1) % 2][0:p, :], scr[0:p, :], C1, C1,
                                ALU.add, ALU.subtract)
                v.tensor_scalar(y1f[(ui + 1) % 2][0:p, :], scr[0:p, :], C2, C2,
                                ALU.add, ALU.subtract).then_inc(VQ, 1)
            v_t2 = einc("VQ")
            @block.gpsimd
            def _(g, ui=ui, p=p, v_t2=v_t2):
                g.wait_ge(VQ, v_t2)
                g.tensor_tensor(scr[0:p, :], y1f[(ui + 1) % 2][0:p, :],
                                outt[(ui + 1) % 2][0:p, :],
                                ALU.subtract).then_inc(GQ, 1)
            g_w = einc("GQ")
            @block.vector
            def _(v, ui=ui, p=p, g_w=g_w):
                v.wait_ge(GQ, g_w)
                if ui >= 2:
                    v.wait_ge(DXQ, 16 * (ui - 1))
                v.scalar_tensor_tensor(xqt[ui % 2][0:p, :], scr[0:p, :],
                                       cols[0:p, 1:2],
                                       outt[(ui + 1) % 2][0:p, :],
                                       ALU.mult, ALU.add).then_inc(VQ, 1)
            v_ln1_ev[ui] = einc("VQ")
            @block.sync
            def _(sy, ui=ui, k=k, p=p):
                sy.wait_ge(VQ, v_ln1_ev[ui])
                if p == 128:
                    sy.dma_start(out=xq_dd[1 + 128 * k:1 + 128 * k + 128, :],
                                 in_=xqt[ui % 2][:, :]).then_inc(DXQ, 16)
                else:
                    sy.dma_start(out=xq_dd[0:SH:2049, :],
                                 in_=xqt[ui % 2][0:2, :]).then_inc(DXQ, 16)
            v_xqstage.append(dinc("DXQ"))
        v_ln1_done = v_ln1_ev[len(units) - 1]

        @block.tensor
        def _(te):
            te.wait_ge(VQ, v_ln1_done)
            te.transpose(PS[4][0:16, 0:128], betac[:, 0:16], identf[:, :])
            te.transpose(PS[5][0:1, 0:2], bhalo[0:2, 0:1],
                         identf[0:2, 0:2]).then_inc(TQ, 1)
        t_bT = einc("TQ")
        @block.vector
        def _(v):
            v.wait_ge(TQ, t_bT)
            v.tensor_copy(betacT[0:16, :], PS[4][0:16, 0:128])
            v.tensor_copy(bhrow[0:1, 0:2], PS[5][0:1, 0:2]).then_inc(VQ, 1)
        v_bT = einc("VQ")
        @block.sync
        def _(sy):
            sy.wait_ge(VQ, v_bT)
            sy.dma_start(out=brow_d[0:1, 1:2049], in_=betacT[0:16, :]).then_inc(DL, 16)
            sy.dma_start(out=brow_d[0:1, 0:SH:2049],
                         in_=bhrow[0:1, 0:2],
                         allow_slow_non_contiguous=True).then_inc(DL, 16)
        dinc("DL")
        v_brow = dinc("DL")

        @block.sync
        def _(sy, dxq_all=cur("DXQ")):
            sy.wait_ge(DXQ, dxq_all)
            sy.dma_start(out=xq_dd[2050:2064, :], in_=zt14[0:14, :]).then_inc(DL, 16)
        v_xqpad = dinc("DL")
        @block.sync
        def _(sy):
            sy.wait_ge(DL, v_xqpad)
            for j in range(8):
                sy.dma_start_transpose(
                    fmB[j][:, :], xq_dd[:, 128 * j:128 * j + 128]).then_inc(DL, 16)
        for j in range(8):
            dinc("DL")
        @block.sync
        def _(sy, tot=cur("DL")):
            sy.wait_ge(DL, tot)
            sy.sem_inc(BR, 1)
        br3 = einc("BR")

        if stage == 3:
            @block.gpsimd
            def _(g):
                g.wait_ge(BR, br3)
                g.dma_start(out=dbg2_d, in_=xq_dd).then_inc(DG, 16)
                g.dma_start(out=dbg_d[:, 0:16], in_=betac[:, 0:16]).then_inc(DG, 16)
                g.dma_start(out=dbg_d[:, 18:20], in_=cols[:, 2:4]).then_inc(DG, 16)
                g.dma_start(out=dbg_d[0:2, 20:21], in_=bhalo[0:2, 0:1],
                            allow_slow_non_contiguous=True).then_inc(DG, 16)
                g.dma_start(out=dbg_d[0:1, 22:24], in_=bhrow[0:1, 0:2]).then_inc(DG, 16)
                g.dma_start(out=dbg_d[0:1, 24:26],
                            in_=brow_d[0:1, 0:SH:2049],
                            allow_slow_non_contiguous=True).then_inc(DG, 16)
                dinc("DG"); dinc("DG"); dinc("DG"); dinc("DG"); dinc("DG"); dinc("DG")
                g.wait_ge(DG, cur("DG"))
                g.wait_ge(DL, cur("DL"))
            return nc


        WY = WD + 1
        v_w2ev = {}
        g_store = []
        for w in range(NW):
            lo = max(256 * w - 1, 0)
            hi = min(256 * w + WD + 2, SH)
            width = hi - lo
            c0h = 256 * w - lo
            @block.sync
            def _(sy, w=w, lo=lo, width=width):
                if w == 0:
                    sy.wait_ge(DL, v_brow)
                else:
                    sy.wait_ge(TQ, t_bw_prev)
                sy.dma_start(out=srow[0:1, 0:width],
                             in_=brow_d[0:1, lo:lo + width]).then_inc(DSR, 16)
            v_srw = dinc("DSR")
            @block.tensor
            def _(te, w=w, lo=lo, width=width, v_srw=v_srw):
                if w == 0:
                    te.wait_ge(BR, br3)
                    te.wait_ge(VQ, v_wq_done)
                te.wait_ge(DSR, v_srw)
                te.matmul(PS[6][:, 0:width], monec[0:1, :], srow[0:1, 0:width],
                          start=True, stop=True).then_inc(TQ, 1)
            t_bw = einc("TQ")
            t_bw_prev = t_bw
            @block.vector
            def _(v, t_bw=t_bw, width=width):
                v.wait_ge(TQ, t_bw)
                v.tensor_copy(Bwin[:, 0:width], PS[6][:, 0:width]).then_inc(VQ, 1)
            v_bw = einc("VQ")

            hbufs = [hbuf, hbuf2]
            tc3s = [tc3, tc2]
            v_e1 = {}
            v_mm = {}
            s_cv_hist = {}
            for c in range(NCH):
                bk = c % 2
                need = v_e1.get(c - 2)
                @block.tensor
                def _(te, w=w, c=c, bk=bk, lo=lo, width=width, need=need,
                      v_bw=v_bw):
                    if c == 0:
                        te.wait_ge(VQ, v_bw)
                        if w == 0:
                            te.wait_ge(BR, br3)
                    if need is not None:
                        te.wait_ge(VQ, need)
                    ins = None
                    for dj in range(8):
                        ins = te.matmul(PS[bk][:, 0:width],
                                        w1q8[dj][:, 128 * c:128 * c + 128],
                                        fmB[dj][:, lo:lo + width],
                                        start=(dj == 0), stop=(dj == 7))
                    ins.then_inc(TQ, 1)
                t_g = einc("TQ")
                @block.vector
                def _(v, w=w, c=c, bk=bk, width=width, c0h=c0h, t_g=t_g,
                      s_si_prev=s_cv_hist.get(c - 2)):
                    v.wait_ge(TQ, t_g)
                    if s_si_prev is not None:
                        v.wait_ge(SQs, s_si_prev)
                    hb = hbufs[c % 2]
                    v.tensor_tensor(hb[:, 0:width], PS[bk][:, 0:width],
                                    Bwin[:, 0:width], ALU.mult)
                    if c0h >= 1:
                        v.tensor_scalar(tc1[:, 0:WY], hb[:, c0h - 1:c0h - 1 + WY],
                                        cwcol[:, 3 * c:3 * c + 1], None, ALU.mult)
                    else:
                        v.tensor_scalar(tc1[:, 1:WY], hb[:, 0:WY - 1],
                                        cwcol[:, 3 * c:3 * c + 1], None, ALU.mult)
                        v.tensor_scalar(tc1[:, 0:1], hb[:, 0:1],
                                        cwcol[:, 3 * c:3 * c + 1], None, ALU.mult)
                    v.scalar_tensor_tensor(tc1[:, 0:WY], hb[:, c0h:c0h + WY],
                                           cwcol[:, 3 * c + 1:3 * c + 2],
                                           tc1[:, 0:WY], ALU.mult, ALU.add)
                    v.scalar_tensor_tensor(tc3s[c % 2][:, 0:WY],
                                           hb[:, c0h + 1:c0h + 1 + WY],
                                           cwcol[:, 3 * c + 2:3 * c + 3],
                                           tc1[:, 0:WY], ALU.mult,
                                           ALU.add).then_inc(VQ, 1)
                v_e1[c] = einc("VQ")
                @block.scalar
                def _(s, c=c, v_e1c=v_e1[c]):
                    s.wait_ge(VQ, v_e1c)
                    s.activation(ybuf[:, WY * c:WY * c + WY],
                                 tc3s[c % 2][:, 0:WY], AF.Silu,
                                 bias=cbcol[:, c:c + 1]).then_inc(SQs, 1)
                s_si = einc("SQ")
                s_cv_hist[c] = s_si
                if c >= 1:
                    cm = c - 1
                    @block.vector
                    def _(v, cm=cm, s_si=s_cv_hist[cm]):
                        v.wait_ge(SQs, s_si)
                        ys = ybuf[:, WY * cm:WY * cm + WY]
                        if cm == 0:
                            v.tensor_copy(Mx[:, 0:WY], ys)
                            v.tensor_copy(Mn[:, 0:WY], ys).then_inc(VQ, 1)
                        else:
                            v.tensor_tensor(Mx[:, 0:WY], Mx[:, 0:WY], ys, ALU.max)
                            v.tensor_tensor(Mn[:, 0:WY], Mn[:, 0:WY], ys,
                                            ALU.min).then_inc(VQ, 1)
                    v_mm[cm] = einc("VQ")
            cm = NCH - 1
            @block.vector
            def _(v, cm=cm, s_si=s_cv_hist[cm]):
                v.wait_ge(SQs, s_si)
                ys = ybuf[:, WY * cm:WY * cm + WY]
                v.tensor_tensor(Mx[:, 0:WY], Mx[:, 0:WY], ys, ALU.max)
                v.tensor_tensor(Mn[:, 0:WY], Mn[:, 0:WY], ys,
                                ALU.min).then_inc(VQ, 1)
            v_mm[cm] = einc("VQ")
            @block.vector
            def _(v, vmm=v_mm[NCH - 1]):
                v.wait_ge(VQ, vmm)
                v.tensor_scalar(Mn[:, 0:WY], Mn[:, 0:WY], -1.0, None, ALU.mult)
                v.tensor_tensor(Mx[:, 0:WY], Mx[:, 0:WY], Mn[:, 0:WY],
                                ALU.max).then_inc(VQ, 1)
            v_am = einc("VQ")
            t_mT_last = None
            for b3, (cw3, wd3) in enumerate(((0, 128), (128, 128), (256, 1))):
                @block.tensor
                def _(te, b3=b3, cw3=cw3, wd3=wd3, v_am=v_am):
                    if b3 == 0:
                        te.wait_ge(VQ, v_am)
                    else:
                        te.wait_ge(VQ, cur("VQ"))
                    te.transpose(PS[6][0:wd3, 0:128], Mx[:, cw3:cw3 + wd3],
                                 identf[:, :]).then_inc(TQ, 1)
                t_mT = einc("TQ")
                @block.vector
                def _(v, b3=b3, wd3=wd3, t_mT=t_mT, w=w, prev_aw=(None if w == 0
                                                                  else v_aw_prev)):
                    v.wait_ge(TQ, t_mT)
                    if b3 == 0 and prev_aw is not None:
                        v.wait_ge(DGA, prev_aw)
                    v.reduce_max(amx[0:wd3, b3:b3 + 1], PS[6][0:wd3, 0:128],
                                 AX).then_inc(VQ, 1)
                einc("VQ")
            for b3 in range(2):
                @block.tensor
                def _(te, b3=b3):
                    te.wait_ge(VQ, cur("VQ"))
                    te.transpose(PS[6][0:1, 0:128], amx[:, b3:b3 + 1],
                                 identf[:, :]).then_inc(TQ, 1)
                t_aT = einc("TQ")
                @block.vector
                def _(v, b3=b3, t_aT=t_aT):
                    v.wait_ge(TQ, t_aT)
                    v.tensor_copy(aseg[0:1, 128 * b3:128 * b3 + 128],
                                  PS[6][0:1, 0:128]).then_inc(VQ, 1)
                einc("VQ")
            @block.vector
            def _(v):
                v.tensor_copy(aseg[0:1, 256:257], amx[0:1, 2:3])
                v.tensor_scalar(s2row[0:1, 0:256], aseg[0:1, 0:256], QCLIP,
                                None, ALU.max)
                v.tensor_scalar(s2row[0:1, 256:257], amx[0:1, 2:3], QCLIP,
                                None, ALU.max).then_inc(VQ, 1)
            v_s2 = einc("VQ")
            @block.gpsimd
            def _(g, v_s2=v_s2):
                g.wait_ge(VQ, v_s2)
                a2, b2 = broadcast_tensor_aps(s2row[0:1, 0:257], cc[0:1, 3:4])
                g.tensor_tensor(s2row[0:1, 0:257], a2, b2,
                                ALU.mult).then_inc(GQ, 1)
            g_s2c = einc("GQ")
            @block.vector
            def _(v, g_s2c=g_s2c):
                v.wait_ge(GQ, g_s2c)
                v.reciprocal(s2row[0:1, 0:257],
                             s2row[0:1, 0:257]).then_inc(VQ, 1)
            g_s2f = ("VQ", einc("VQ"))
            @block.sync
            def _(sy, w=w, v_s2=v_s2):
                sy.wait_ge(VQ, v_s2)
                sy.dma_start(out=arow_d[0:1, 256 * w:256 * w + 257],
                             in_=aseg[0:1, 0:257]).then_inc(DGA, 16)
            v_aw = dinc("DGA")
            v_aw_prev = v_aw
            @block.sync
            def _(sy, w=w, v_aw=v_aw, gprev=(None if w == 0 else v_w2ev.get((w - 1, 1)))):
                sy.wait_ge(DGA, v_aw)
                if gprev is not None:
                    sy.wait_ge(VQ, gprev)
                sy.dma_start(
                    out=gcols[:, 0:2],
                    in_=arow_f[1 + 256 * w:1 + 256 * w + 256].rearrange(
                        "(j p) -> p j", p=128),
                    allow_slow_non_contiguous=True).then_inc(DGC, 16)
            v_gc = dinc("DGC")
            @block.tensor
            def _(te, g_s2f=g_s2f):
                te.wait_ge(VQ, g_s2f[1])
                te.matmul(PS[6][:, 0:257], monec[0:1, :], s2row[0:1, 0:257],
                          start=True, stop=True).then_inc(TQ, 1)
            t_s2b = einc("TQ")
            @block.vector
            def _(v, t_s2b=t_s2b):
                v.wait_ge(TQ, t_s2b)
                v.tensor_copy(S2t[:, 0:WY], PS[6][:, 0:257]).then_inc(VQ, 1)
            v_s2t = einc("VQ")

            v_xq2 = {}
            t_w2 = {}
            for c in range(NCH):
                @block.vector
                def _(v, c=c, v_s2t=v_s2t, s_si_c=s_cv_hist[c]):
                    v.wait_ge(VQ, v_s2t)
                    v.wait_ge(SQs, s_si_c)
                    if c >= 2:
                        v.wait_ge(TQ, t_w2[c - 2])
                    v.tensor_tensor(pq[:, 0:WY], ybuf[:, WY * c:WY * c + WY],
                                    S2t[:, 0:WY], ALU.mult)
                    v.tensor_scalar(qr32[:, 0:WY], pq[:, 0:WY], C1, C1,
                                    ALU.add, ALU.subtract)
                    v.tensor_scalar(qr64[:, 0:WY], pq[:, 0:WY], C2, C2,
                                    ALU.add, ALU.subtract)
                    v.tensor_tensor(qwt[:, 0:WY], qr64[:, 0:WY], qr32[:, 0:WY],
                                    ALU.subtract)
                    v.scalar_tensor_tensor(xq2[c % 2][:, 0:WY], qwt[:, 0:WY],
                                           cols[:, 1:2], qr32[:, 0:WY],
                                           ALU.mult, ALU.add).then_inc(VQ, 1)
                v_xq2[c] = einc("VQ")
                @block.tensor
                def _(te, w=w, c=c, v_xq2c=v_xq2[c]):
                    te.wait_ge(VQ, v_xq2c)
                    if c == 0:
                        if w > 0:
                            te.wait_ge(VQ, v_w2ev[(w - 1, 1)])
                    ins = None
                    for j in range(2):
                        lh = xq2[c % 2][:, 1 + 128 * j:1 + 128 * j + 128]
                        te.matmul(PS[2 + 2 * j][:, 0:512], lh, w2q8[c][:, 0:512],
                                  start=(c == 0), stop=(c == NCH - 1))
                        ins = te.matmul(PS[3 + 2 * j][:, 0:512], lh,
                                        w2q8[c][:, 512:1024],
                                        start=(c == 0), stop=(c == NCH - 1))
                    ins.then_inc(TQ, 1)
                t_w2[c] = einc("TQ")
            t_w2_done = t_w2[NCH - 1]

            @block.vector
            def _(v, v_gc=v_gc):
                v.wait_ge(DGC, v_gc)
                v.tensor_scalar(gcols[:, 0:2], gcols[:, 0:2], QCLIP, None,
                                ALU.max).then_inc(VQ, 1)
            v_gm = einc("VQ")
            @block.scalar
            def _(s, v_gm=v_gm):
                s.wait_ge(VQ, v_gm)
                s.activation(gcols[:, 0:2], gcols[:, 0:2], AF.Identity,
                             scale=cols[:, 24:25]).then_inc(SQs, 1)
            v_g = einc("SQ")

            for j in range(2):
                m = 2 * w + j
                @block.vector
                def _(v, w=w, j=j, m=m, t_w2_done=t_w2_done, v_g=v_g):
                    v.wait_ge(TQ, t_w2_done)
                    v.wait_ge(SQs, v_g)
                    v.scalar_tensor_tensor(y1f[j][:, 0:512], PS[2 + 2 * j][:, 0:512],
                                           gcols[:, j:j + 1], y1t[m][:, 0:512],
                                           ALU.mult, ALU.add,
                                           accum_out=cols[:, 10:11])
                    v.scalar_tensor_tensor(y1f[j][:, 512:1024],
                                           PS[3 + 2 * j][:, 0:512],
                                           gcols[:, j:j + 1], y1t[m][:, 512:1024],
                                           ALU.mult, ALU.add,
                                           accum_out=cols[:, 9:10]).then_inc(VQ, 1)
                v_y2 = einc("VQ")
                @block.scalar
                def _(s, j=j, v_y2=v_y2):
                    s.wait_ge(VQ, v_y2)
                    s.activation(scr[:, :], y1f[j][:, :], AF.Square,
                                 accum_out=cols[:, 11:12]).then_inc(SQs, 1)
                s_sq2 = einc("SQ")
                @block.gpsimd
                def _(g, s_sq2=s_sq2):
                    g.wait_ge(SQs, s_sq2)
                    g.tensor_tensor(cols[:, 12:13], cols[:, 10:11],
                                    cols[:, 9:10], ALU.add)
                    g.tensor_tensor(cols[:, 12:13], cols[:, 12:13],
                                    cc[:, 4:5], ALU.mult)
                    g.tensor_tensor(cols[:, 26:27], cols[:, 12:13],
                                    cols[:, 12:13], ALU.mult)
                    g.tensor_tensor(cols[:, 11:12], cols[:, 11:12],
                                    cc[:, 4:5], ALU.mult)
                    g.tensor_tensor(cols[:, 13:14], cols[:, 11:12],
                                    cols[:, 26:27], ALU.subtract)
                    g.tensor_tensor(cols[:, 13:14], cols[:, 13:14],
                                    cc[:, 5:6], ALU.add).then_inc(GQ, 1)
                g_l2 = einc("GQ")
                @block.scalar
                def _(s, g_l2=g_l2):
                    s.wait_ge(GQ, g_l2)
                    s.activation(cols[:, 27:28], cols[:, 13:14],
                                 AF.Sqrt).then_inc(SQs, 1)
                s_a2 = einc("SQ")
                @block.vector
                def _(v, s_a2=s_a2):
                    v.wait_ge(SQs, s_a2)
                    v.reciprocal(cols[:, 14:15],
                                 cols[:, 27:28]).then_inc(VQ, 1)
                v_a2 = einc("VQ")
                @block.gpsimd
                def _(g, v_a2=v_a2):
                    g.wait_ge(VQ, v_a2)
                    g.tensor_tensor(cols[:, 15:16], cols[:, 12:13],
                                    cols[:, 14:15], ALU.mult)
                    g.tensor_tensor(cols[:, 15:16], cols[:, 15:16],
                                    cols[:, 29:30], ALU.mult).then_inc(GQ, 1)
                g_b2 = einc("GQ")
                @block.scalar
                def _(s, w=w, j=j, g_b2=g_b2):
                    s.wait_ge(GQ, g_b2)
                    if len(g_store) >= 2:
                        s.wait_ge(DG, g_store[len(g_store) - 2])
                    s.activation(outt[j][:, :], y1f[j][:, :], AF.Identity,
                                 scale=cols[:, 14:15],
                                 bias=cols[:, 15:16]).then_inc(SQs, 1)
                s_out = einc("SQ")
                v_w2ev[(w, j)] = None
                @block.vector
                def _(v, s_out=s_out):
                    v.wait_ge(SQs, s_out)
                    v.memset(cols[0:1, 31:32], 0.0).then_inc(VQ, 1)
                v_w2ev[(w, j)] = einc("VQ")
                @block.gpsimd
                def _(g, m=m, j=j, s_out=s_out):
                    g.wait_ge(SQs, s_out)
                    g.dma_start(out=out_d[128 * m:128 * m + 128, :],
                                in_=outt[j][:, :]).then_inc(DG, 16)
                g_store.append(dinc("DG"))

        if stage == 5:
            @block.gpsimd
            def _(g):
                g.wait_ge(DG, g_store[-1])
                g.dma_start(out=dbg_d[:, 0:264],
                            in_=ybuf[:, 7960:8224]).then_inc(DG, 16)
                g.dma_start(out=dbg_d[:, 264:523],
                            in_=hbuf[:, 0:259]).then_inc(DG, 16)
                g.dma_start(out=dbg_d[:, 523:782],
                            in_=Bwin[:, 0:259]).then_inc(DG, 16)
                g.dma_start(out=dbg2_d[0:1, 0:269],
                            in_=arow_d[0:1, 1780:2049]).then_inc(DG, 16)
                dinc("DG"); dinc("DG"); dinc("DG"); dinc("DG")
                g.wait_ge(DG, cur("DG"))
                g.wait_ge(DL, cur("DL"))
                g.wait_ge(DGA, cur("DGA"))
                g.wait_ge(DGC, cur("DGC"))
                g.wait_ge(DSR, cur("DSR"))
            return nc

        @block.gpsimd
        def _(g):
            g.wait_ge(DG, g_store[-1])
            g.wait_ge(DL, cur("DL"))
            g.wait_ge(DGA, cur("DGA"))
            g.wait_ge(DGC, cur("DGC"))
            g.wait_ge(DSR, cur("DSR"))

    return nc


_PROG_CACHE = {}


def _get_prog(stage=4):
    if stage not in _PROG_CACHE:
        _PROG_CACHE[stage] = build_program(stage)
    return _PROG_CACHE[stage]


def kernel(L0_attn_query, L0_attn_Wq, L0_attn_Wk, L0_attn_Wv, L0_attn_Wo,
           L0_ffn_w1, L0_ffn_conv_w, L0_ffn_conv_b, L0_ffn_w2, _stage=4):
    x = np.ascontiguousarray(L0_attn_query)
    w1t = np.ascontiguousarray(np.asarray(L0_ffn_w1).T)
    w2t = np.ascontiguousarray(np.asarray(L0_ffn_w2).T)
    cw = np.ascontiguousarray(np.asarray(L0_ffn_conv_w).reshape(DHID, 3))
    cb = np.ascontiguousarray(L0_ffn_conv_b)
    wo = np.ascontiguousarray(L0_attn_Wo)
    wqa = np.ascontiguousarray(L0_attn_Wq)
    wka = np.ascontiguousarray(L0_attn_Wk)
    wva = np.ascontiguousarray(L0_attn_Wv)

    in_maps = []
    for c in range(8):
        b, qd = c // 4, c % 4
        lo = 2048 * qd - 1
        hi = 2048 * qd + 2049
        rows = np.arange(lo, hi)
        np.clip(rows, 0, 8191, out=rows)
        xs = np.ascontiguousarray(x[b][rows])
        mask = np.zeros((128, 3), np.float32)
        mask[:, 0] = 0.0 if qd == 0 else 1.0
        mask[:, 1] = 0.0 if qd == 3 else 1.0
        mask[:, 2] = 1.0
        mask[0, 2] = 0.0 if qd == 0 else 1.0
        mask[1, 2] = 0.0 if qd == 3 else 1.0
        in_maps.append({"x": xs, "wq": wqa, "wk": wka, "wv": wva, "wo": wo,
                        "w1t": w1t, "w2t": w2t, "cw": cw, "cb": cb,
                        "mask": mask})

    prog = _get_prog(_stage)
    res = run_bass_kernel_spmd(prog, in_maps, core_ids=list(range(8)))
    if _stage != 4:
        return res
    out = np.empty((2, 8192, 1024), np.float32)
    for c in range(8):
        b, qd = c // 4, c % 4
        out[b, 2048 * qd:2048 * qd + 2048, :] = res.results[c]["out"]
    return out
